# version 09
# baseline (speedup 1.0000x reference)
import numpy as np
import ml_dtypes
import concourse.bass as bass
import concourse.mybir as mybir
from concourse.bass_utils import run_bass_kernel_spmd
from contextlib import ExitStack

F32 = mybir.dt.float32
BF16 = mybir.dt.bfloat16
AF = mybir.ActivationFunctionType
ALU = mybir.AluOpType
AX = mybir.AxisListType

B, T, D = 4, 2048, 2048
H, DK, DV = 6, 256, 512
TW = 1024
TP = TW + 128
NT = TP // 128
NPAIR = TW // 128
KC = D // 128
NFQ, NFK, NFV, NFG = 12, 12, 24, 24
DHID = 8192
EPS = 1e-5

bfarr = lambda x: np.ascontiguousarray(np.asarray(x)).astype(ml_dtypes.bfloat16)


class Sem:
    __slots__ = ("h", "n", "name")

    def __init__(self, ctx, nc, name):
        self.h = ctx.enter_context(nc.semaphore(name))
        self.n = 0
        self.name = name


class Plan:

    def __init__(self):
        self.ops = []

    def on(self, eng, fn, wait=(), inc=(), wait_at=()):
        waits = [(s, s.n) for s in wait if s.n > 0]
        waits += [(s, v) for (s, v) in wait_at if v > 0]
        incs = []
        for item in inc:
            s, k = item if isinstance(item, tuple) else (item, 1)
            s.n += k
            incs.append((s, k))
        self.ops.append((eng, fn, waits, incs))

    def realize(self, nc):
        ops = self.ops
        self.ops = []
        with nc.Block() as blk:
            def run_for(name):
                def body(e):
                    seen = {}
                    selfcnt = {}
                    for eng, fn, waits, incs in ops:
                        if eng != name:
                            continue
                        for s, v in waits:
                            if seen.get(s, -1) >= v:
                                continue
                            if selfcnt.get(s, 0) >= v:
                                pass
                            e.wait_ge(s.h, v)
                            seen[s] = v
                        if fn is not None:
                            ins = fn(e)
                            for s, k in incs:
                                ins.then_inc(s.h, k)
                                selfcnt[s] = selfcnt.get(s, 0) + k
                return body
            blk.sync(run_for("sync"))
            blk.tensor(run_for("tensor"))
            blk.vector(run_for("vector"))
            blk.scalar(run_for("scalar"))
            blk.gpsimd(run_for("gpsimd"))


class KB:
    def __init__(self, debug=False, stop_after=None):
        self.nc = bass.Bass("TRN2", target_bir_lowering=False)
        self.ctx = ExitStack()
        self.p = Plan()
        self.debug = debug
        self.stop_after = stop_after

    _name_ctr = {}

    def sb(self, scope, name, shape, dtype, side=None):
        n = KB._name_ctr.get(name, 0)
        KB._name_ctr[name] = n + 1
        uname = name if n == 0 else f"{name}_{n}"
        kw = {"side": side} if side else {}
        t = scope.enter_context(self.nc.sbuf_tensor(uname, shape, dtype, **kw))
        setattr(self, name, t)
        return t

    def sem(self, name):
        s = Sem(self.ctx, self.nc, name)
        setattr(self, name, s)
        return s

    def din(self, name, shape, dtype=F32):
        t = self.nc.dram_tensor(name, list(shape), dtype, kind="ExternalInput").ap()
        setattr(self, name, t)
        return t

    def dout(self, name, shape, dtype=F32):
        t = self.nc.dram_tensor(name, list(shape), dtype, kind="ExternalOutput").ap()
        setattr(self, name, t)
        return t

    def dint(self, name, shape, dtype=F32):
        kind = "ExternalOutput" if self.debug else "Internal"
        t = self.nc.dram_tensor(name, list(shape), dtype, kind=kind).ap()
        setattr(self, name, t)
        return t

    def pe_mm(self, out, lhsT, rhs, bank, start, stop, wait=(), inc_pe=True):
        wa = [self.bank_last[bank]] if start else []
        self.p.on("tensor",
                  lambda e: e.matmul(out=out, lhsT=lhsT, rhs=rhs, start=start, stop=stop),
                  wait=wait, wait_at=wa, inc=([self.s_pe] if (inc_pe and stop) else ()))

    def pe_tr(self, out, in_, ident, bank, wait=()):
        self.p.on("tensor", lambda e: e.transpose(out=out, in_=in_, identity=ident),
                  wait=wait, wait_at=[self.bank_last[bank]], inc=[self.s_pe])

    def evac(self, eng, fn, bank, wait=(), inc=(), sem=None):
        if sem is None:
            sem = self.s_v if eng == "vector" else self.s_sc
        self.p.on(eng, fn, wait=[self.s_pe] + list(wait), inc=[sem] + list(inc))
        self.bank_last[bank] = (sem, sem.n)

    def fence(self, engines=("tensor", "vector", "scalar", "gpsimd", "sync")):
        allsems = [self.s_pe, self.s_v, self.s_sc, self.s_gd, self.s_ld,
                   self.s_w, self.s_v0, self.s_v1, self.s_c0, self.s_c1]
        for e in engines:
            self.p.on(e, None, wait=allsems)

    def dbg_dump(self, ap):
        if not self.debug or self._dbg_i >= 32:
            return
        i = self._dbg_i; self._dbg_i += 1
        s = self
        self.p.on("vector", lambda e, ap=ap, i=i: e.tensor_copy(
            out=s.ln_scr[:, 0:ap.shape[-1]] if ap.shape[-1] <= D else None, in_=ap),
                  wait=[s.s_pe, s.s_v, s.s_sc], inc=[s.s_v])
        self.p.on("gpsimd", lambda e, ap=ap, i=i: e.dma_start(
            out=s.dbg_t[i, 0:ap.shape[0], 0:ap.shape[-1]],
            in_=s.ln_scr[0:ap.shape[0], 0:ap.shape[-1]]),
                  wait=[s.s_v], inc=[(s.s_gd, 16)])
        self.p.on("vector", None, wait=[s.s_gd])

    def finish_early(self, nc):
        self.fence()
        self.p.realize(nc)
        return nc

    def build(self):
        nc = self.nc
        self.declare_io()
        s0 = self.ctx
        self.alloc_persistent(s0)
        ln_scope = ExitStack()
        self.alloc_ln(ln_scope)

        self.emit_startup()
        self.p.realize(nc)
        if self.stop_after == "startup":
            return self.finish_early(nc)

        s1 = ExitStack()
        self.alloc_scan(s1)
        for w in (0, 1):
            s2 = ExitStack()
            self.alloc_window(s2)
            self.emit_window_proj(w)
            self.p.realize(nc)
            if self.stop_after == f"w{w}proj":
                return self.finish_early(nc)
            self.emit_scan(w)
            self.p.realize(nc)
            self.fence(("tensor", "vector", "scalar", "gpsimd"))
            self.p.realize(nc)
            s2.close()
            if self.stop_after and self.stop_after.startswith(f"w{w}scan"):
                return self.finish_early(nc)
        s1.close()

        sDE = ExitStack()
        self.sb(sDE, "h2T", [128, KC * TW], BF16)
        s3 = ExitStack()
        self.alloc_phaseD(s3)
        self.emit_phaseD()
        self.p.realize(nc)
        self.fence(("tensor", "vector", "scalar", "gpsimd"))
        self.p.realize(nc)
        s3.close()
        if self.stop_after == "phaseD":
            return self.finish_early(nc)
        ln_scope.close()

        s4 = ExitStack()
        self.alloc_ffn(s4)
        self.emit_ffnA()
        self.p.realize(nc)
        self.fence(("tensor", "vector", "scalar", "gpsimd"))
        self.p.realize(nc)
        s4.close()
        sDE.close()
        s5 = ExitStack()
        self.alloc_ffnB(s5)
        self.emit_ffnB()
        self.emit_final()
        self.p.realize(nc)
        s5.close()
        return nc

    def declare_io(self):
        d = self.din
        d("xw", [2, TP, D])
        d("c_r", [16, 128])
        d("maskpre", [128, 1])
        d("maskhalo", [128, 1])
        d("wq", [D, 1536]); d("wk", [D, 1536]); d("wv", [D, 3072])
        d("wab", [D, 16])
        d("wg", [D, 3072])
        d("wo", [3072, D])
        d("wmod", [D, 6 * D])
        d("wgt", [D, DHID]); d("wut", [D, DHID]); d("wdt", [DHID, D])
        d("convw", [128, 48 * 4])
        d("alog", [1, 8]); d("dtbv", [1, 8])
        d("normw", [1, DV])
        d("ident", [128, 128]); d("identb", [128, 128], BF16)
        d("cumu", [128, 64])
        d("masks_c", [128, 128]); d("maski_c", [128, 128])
        d("onesf", [1, 128])
        d("constp", [128, 8])
        self.dout("out", [TW, D])
        self.dint("mod_d", [1, 6 * D])
        self.dint("kf_d", [NFK, 128, TW], BF16)
        self.dint("qf_d", [NFQ, 128, TW], BF16)
        self.dint("vtok_d", [NPAIR, 128, 3072], BF16)
        self.dint("gate_d", [NPAIR, 128, 3072], BF16)
        self.dint("o_d", [NPAIR, 128, 3072], BF16)
        self.dint("gu_d", [DHID, TW], BF16)
        self.dint("x1_d", [NPAIR, 128, D])
        if self.debug:
            self.dout("hT_dbg", [2, 128, KC * TP], BF16)
            self.dout("bc_dbg", [2, 128, D])
            self.dout("dbg_t", [32, 128, 512])
            self._dbg_i = 0


    def alloc_persistent(self, sc):
        sb = self.sb
        sb(sc, "id_f", [128, 128], F32)
        sb(sc, "id_b", [128, 128], BF16)
        sb(sc, "cum_u", [128, 64], F32)
        sb(sc, "mskS", [128, 128], F32)
        sb(sc, "mskI", [128, 128], F32)
        sb(sc, "ones128", [128, 128], F32)
        sb(sc, "ones_f", [1, 128], F32)
        sb(sc, "cpool", [128, 8], F32)
        sb(sc, "mskpre_t", [128, 1], F32)
        sb(sc, "mskhalo_t", [128, 1], F32)
        sb(sc, "normw_t", [128, DV], F32)
        sb(sc, "dtb_t", [128, 8], F32)
        sb(sc, "negA_t", [128, 8], F32)
        sb(sc, "convw_t", [128, 192], F32)
        sb(sc, "wab_b", [128, 16 * 16], BF16)
        sb(sc, "silu_ct", [128, 16], BF16)
        sb(sc, "modrow", [1, 512], F32)
        sb(sc, "bcA", [128, D], F32)
        sb(sc, "bcB", [128, D], F32)
        sb(sc, "pan_f", [128, 2 * 2048], F32)
        sb(sc, "pan_b", [128, 2 * 2048], BF16)
        sb(sc, "smallp", [128, 32], F32)
        self.pbank = [sc.enter_context(self.nc.psum_tensor(f"pb{i}", [128, 512], F32))
                      for i in range(6)]
        self.pt = [sc.enter_context(self.nc.psum_tensor(f"pt{i}", [128, 1024], BF16))
                   for i in range(2)]
        for nm in ("s_pe", "s_v", "s_sc", "s_sy", "s_gd", "s_ld", "s_w",
                   "s_v0", "s_v1", "s_c0", "s_c1"):
            self.sem(nm)
        self.bank_last = [(self.s_v, 0) for _ in range(8)]

    def alloc_ln(self, sc):
        sb = self.sb
        sb(sc, "x_tile", [128, D], F32, side="right")
        sb(sc, "ln_scr", [128, D], F32, side="right")
        sb(sc, "h_tile", [128, D], BF16, side="right")
        sb(sc, "ln_sums", [128, 8], F32, side="right")

    def alloc_scan(self, sc):
        sb = self.sb
        sb(sc, "S_sb", [128, 12 * DV], BF16)
        sb(sc, "kfp", [128, 2 * NFK * 128], BF16)
        sb(sc, "qfp", [128, 2 * NFQ * 128], BF16)
        sb(sc, "vpair", [128, 2 * 3072], BF16)
        sb(sc, "gatep", [128, 2 * 3072], BF16)
        sb(sc, "oasm", [128, 3072], BF16)
        sb(sc, "abraw", [128, NPAIR * 16], F32)
        sb(sc, "bwin", [128, NPAIR * 8], F32)
        sb(sc, "nbwin", [128, NPAIR * 8], F32)
        sb(sc, "gwin", [128, NPAIR * 8], F32)
        sb(sc, "gamp", [128, 8], F32)
        sb(sc, "bp", [128, 8], F32)
        sb(sc, "lamg", [128, 16], F32)
        sb(sc, "lamt", [128, 16], F32)
        sb(sc, "eep", [128, 8], F32)
        sb(sc, "nbb", [128, 8], F32)
        sb(sc, "ktok", [128, 2 * DK], BF16)
        sb(sc, "ketok6", [128, 6 * DK], BF16)
        sb(sc, "nsq", [128, 48], F32)
        sb(sc, "sq_scr", [128, 2 * 256], F32)
        sb(sc, "E_sb", [128, 2 * 128], F32)
        sb(sc, "AX1", [128, 2 * 128], BF16)
        sb(sc, "AXT", [128, 2 * 128], BF16)
        sb(sc, "APT6", [128, 6 * 128], BF16)
        sb(sc, "DqK", [128, 2 * 128], BF16)
        sb(sc, "MuT6", [128, 6 * 128], BF16)
        sb(sc, "lamI", [128, 6 * 256], BF16)
        sb(sc, "bV", [128, 2 * DV], F32)
        sb(sc, "w_sb", [128, 2 * DV], BF16)
        sb(sc, "u_bf", [128, 2 * DV], BF16)
        sb(sc, "t2", [128, 2 * DV], F32)

    def alloc_window(self, sc):
        sb = self.sb
        sb(sc, "hT", [128, KC * TP], BF16)
        sb(sc, "praw", [128, 2 * TP], BF16)
        sb(sc, "cacc", [128, 2 * TW], F32)
        sb(sc, "vasm", [128, NPAIR * 128], BF16)

    def alloc_phaseD(self, sc):
        sb = self.sb
        sb(sc, "oT", [128, 24 * TW], BF16)
        sb(sc, "otile", [128, 3072], BF16)
        sb(sc, "wopan", [128, 24 * 512], BF16)
        sb(sc, "res_t", [128, D], F32)

    def alloc_ffn(self, sc):
        sb = self.sb
        sb(sc, "fpan_f", [128, 2 * 2048], F32)
        sb(sc, "fpan_b", [128, 2 * 2048], BF16)
        sb(sc, "gu_sb", [128, 2 * TW], BF16)

    def alloc_ffnB(self, sc):
        sb = self.sb
        sb(sc, "wdpan", [128, 64 * 512], BF16)
        sb(sc, "gucol", [128, 2 * 64 * 128], BF16)
        sb(sc, "res2_t", [128, 2048], F32)

    def emit_startup(self):
        p, s = self.p, self

        def ld(dst, src, wait=()):
            p.on("sync", lambda e, d=dst, sr=src: e.dma_start(out=d, in_=sr),
                 wait=wait, inc=[(s.s_ld, 16)])

        ld(s.id_f[:], s.ident[:])
        ld(s.id_b[:], s.identb[:])
        ld(s.cum_u[:], s.cumu[:])
        ld(s.mskS[:], s.masks_c[:])
        ld(s.mskI[:], s.maski_c[:])
        ld(s.ones_f[:], s.onesf[:])
        ld(s.cpool[:], s.constp[:])
        ld(s.mskpre_t[:], s.maskpre[:])
        ld(s.mskhalo_t[:], s.maskhalo[:])
        ld(s.convw_t[:], s.convw[:])
        ld(s.ln_scr[0:16, 0:128], s.c_r[:])
        p.on("vector", lambda e: e.memset(s.ones128[:], 1.0), inc=[s.s_v])
        ld(s.ln_scr[0:1, 256:256 + DV], s.normw[:])
        ld(s.ln_scr[0:1, 768:776], s.alog[:])
        ld(s.ln_scr[0:1, 776:784], s.dtbv[:])
        ld(s.pan_f[:, 0:256], s.wab[:].rearrange("(kc p) m -> p kc m", p=128))

        p.on("scalar", lambda e: e.activation(out=s.ln_scr[0:16, 128:256],
                                              in_=s.ln_scr[0:16, 0:128], func=AF.Silu),
             wait=[s.s_ld], inc=[s.s_sc])
        p.on("tensor", lambda e: e.transpose(out=s.pbank[0][:, 0:16],
                                             in_=s.ln_scr[0:16, 128:256],
                                             identity=s.id_f[0:16, 0:16]),
             wait=[s.s_sc], inc=[s.s_pe])
        s.evac("vector", lambda e: e.tensor_copy(out=s.silu_ct[:], in_=s.pbank[0][:, 0:16]),
               bank=0)
        p.on("vector", lambda e: e.tensor_copy(out=s.wab_b[:], in_=s.pan_f[:, 0:256]),
             wait=[s.s_ld], inc=[s.s_v])
        s.pe_mm(s.pbank[1][:, 0:512], s.ones_f[:], s.ln_scr[0:1, 256:256 + DV],
                bank=1, start=True, stop=True, wait=[s.s_ld])
        s.evac("vector", lambda e: e.tensor_copy(out=s.normw_t[:], in_=s.pbank[1][:, 0:512]),
               bank=1)
        s.pe_mm(s.pbank[0][:, 0:8], s.ones_f[:], s.ln_scr[0:1, 768:776],
                bank=0, start=True, stop=True)
        s.pe_mm(s.pbank[0][:, 8:16], s.ones_f[:], s.ln_scr[0:1, 776:784],
                bank=0, start=False, stop=True)
        s.evac("scalar", lambda e: e.activation(out=s.negA_t[:], in_=s.pbank[0][:, 0:8],
                                                func=AF.Exp), bank=0)
        p.on("vector", lambda e: e.tensor_scalar_mul(out=s.negA_t[:], in0=s.negA_t[:],
                                                     scalar1=-1.0),
             wait=[s.s_sc], inc=[s.s_v])
        s.evac("vector", lambda e: e.tensor_copy(out=s.dtb_t[:], in_=s.pbank[0][:, 8:16]),
               bank=0, wait=[s.s_sc])

        def stage_mod(j):
            half = j % 2
            wwar = [s.s_pe] if j >= 2 else [s.s_v]
            p.on("sync", lambda e, j=j, half=half: e.dma_start(
                out=s.pan_f[:, half * 2048:(half + 1) * 2048],
                in_=s.wmod[:, j * 128:(j + 1) * 128].rearrange("(kc p) m -> p kc m", p=128)),
                 wait=wwar, inc=[(s.s_ld, 16)])
            p.on("vector", lambda e, half=half: e.tensor_copy(
                out=s.pan_b[:, half * 2048:(half + 1) * 2048],
                in_=s.pan_f[:, half * 2048:(half + 1) * 2048]),
                 wait=[s.s_ld, s.s_pe], inc=[s.s_w])
        stage_mod(0)
        stage_mod(1)
        for j in range(96):
            half = j % 2
            bank = 2 + half
            for kc in range(KC):
                s.pe_mm(s.pbank[bank][0:1, 0:128], s.silu_ct[:, kc:kc + 1],
                        s.pan_b[:, half * 2048 + kc * 128: half * 2048 + (kc + 1) * 128],
                        bank=bank, start=(kc == 0), stop=(kc == 15),
                        wait=[s.s_w] + ([s.s_v] if j == 0 else []),
                        inc_pe=(kc == 15))
            if j + 2 < 96:
                stage_mod(j + 2)
            s.evac("vector", lambda e, bank=bank, j=j: e.tensor_copy(
                out=s.modrow[0:1, (j % 4) * 128:(j % 4) * 128 + 128],
                in_=s.pbank[bank][0:1, 0:128]),
                   bank=bank, wait=[s.s_gd])
            if j % 4 == 3:
                p.on("gpsimd", lambda e, j=j: e.dma_start(
                    out=s.mod_d[:, (j - 3) * 128:(j + 1) * 128], in_=s.modrow[0:1, 0:512]),
                     wait=[s.s_v], inc=[(s.s_gd, 16)])

    def emit_mod_bcast(self, which, scr=None):
        p, s = self.p, self
        if scr is None:
            scr = self.ln_scr
        def one_vec(src_off, dst, fused):
            for t in range(4):
                p.on("sync", lambda e, t=t: e.dma_start(
                    out=s.modrow[0:1, 0:512],
                    in_=s.mod_d[:, src_off + t * 512:src_off + (t + 1) * 512]),
                     wait=[s.s_sc, s.s_v, s.s_gd], inc=[(s.s_ld, 16)])
                s.pe_mm(s.pbank[4][:, 0:512], s.ones_f[:], s.modrow[0:1, 0:512],
                        bank=4, start=True, stop=True, wait=[s.s_ld])
                fused(t, dst)
        if which in ("a", "b"):
            so = 0 if which == "a" else 3 * D
            co = D if which == "a" else 4 * D
            one_vec(so, s.bcB, lambda t, dst: s.evac(
                "vector", lambda e, t=t: e.tensor_copy(
                    out=dst[:, t * 512:(t + 1) * 512], in_=s.pbank[4][:, 0:512]), bank=4))
            one_vec(co, s.bcA, lambda t, dst: s.evac(
                "vector", lambda e, t=t: e.tensor_scalar_add(
                    out=dst[:, t * 512:(t + 1) * 512], in0=s.pbank[4][:, 0:512],
                    scalar1=1.0), bank=4))
        else:
            go = 2 * D if which == "ga" else 5 * D
            def clamp(t, dst):
                sl = slice(t * 512, (t + 1) * 512)
                s.evac("scalar", lambda e: e.activation(out=scr[:, 0:512],
                                                        in_=s.pbank[4][:, 0:512],
                                                        func=AF.Abs), bank=4)
                s.evac("scalar", lambda e: e.sign(out=scr[:, 512:1024],
                                                  in_=s.pbank[4][:, 0:512]), bank=4)
                p.on("vector", lambda e: e.tensor_scalar_max(
                    out=scr[:, 0:512], in0=scr[:, 0:512], scalar1=0.25),
                     wait=[s.s_sc], inc=[s.s_v])
                p.on("vector", lambda e, sl=sl: e.tensor_mul(
                    out=dst[:, sl], in0=scr[:, 0:512], in1=scr[:, 512:1024]),
                     wait=[s.s_sc], inc=[s.s_v])
            one_vec(go, s.bcA, clamp)

    def emit_ln_tile(self, xsrc, dstT, dst_stride, tok_off, last_readers):
        p, s = self.p, self
        p.on("sync", lambda e: e.dma_start(out=s.x_tile[:], in_=xsrc),
             wait=last_readers, inc=[(s.s_ld, 16)])
        p.on("vector", lambda e: e.reduce_sum(out=s.ln_sums[:, 0:1], in_=s.x_tile[:],
                                              axis=AX.X),
             wait=[s.s_ld], inc=[s.s_v])
        p.on("scalar", lambda e: e.activation(out=s.ln_scr[:], in_=s.x_tile[:],
                                              func=AF.Square,
                                              accum_out=s.ln_sums[:, 1:2]),
             wait=[s.s_ld], inc=[s.s_sc])
        p.on("scalar", lambda e: e.activation(out=s.ln_sums[:, 4:5],
                                              in_=s.ln_sums[:, 1:2], func=AF.Copy,
                                              scale=1.0 / D), inc=[s.s_sc])
        p.on("vector", lambda e: e.tensor_scalar_mul(out=s.ln_sums[:, 2:3],
                                                     in0=s.ln_sums[:, 0:1],
                                                     scalar1=1.0 / D),
             wait=[s.s_v, s.s_sc], inc=[s.s_v])
        p.on("vector", lambda e: e.tensor_mul(out=s.ln_sums[:, 3:4],
                                              in0=s.ln_sums[:, 2:3],
                                              in1=s.ln_sums[:, 2:3]),
             wait=[s.s_v], inc=[s.s_v])
        p.on("vector", lambda e: e.tensor_sub(out=s.ln_sums[:, 5:6],
                                              in0=s.ln_sums[:, 4:5],
                                              in1=s.ln_sums[:, 3:4]),
             wait=[s.s_v, s.s_sc], inc=[s.s_v])
        p.on("scalar", lambda e: e.activation(out=s.ln_sums[:, 6:7], in_=s.ln_sums[:, 5:6],
                                              func=AF.Ln, bias=s.cpool[:, 0:1]),
             wait=[s.s_v], inc=[s.s_sc])
        p.on("scalar", lambda e: e.activation(out=s.ln_sums[:, 7:8], in_=s.ln_sums[:, 6:7],
                                              func=AF.Exp, scale=-0.5), inc=[s.s_sc])
        p.on("vector", lambda e: e.scalar_tensor_tensor(
            out=s.ln_scr[:], in0=s.x_tile[:], scalar=s.ln_sums[:, 2:3], in1=s.bcA[:],
            op0=ALU.subtract, op1=ALU.mult),
             wait=[s.s_sc, s.s_v], inc=[s.s_v])
        p.on("vector", lambda e: e.scalar_tensor_tensor(
            out=s.h_tile[:], in0=s.ln_scr[:], scalar=s.ln_sums[:, 7:8], in1=s.bcB[:],
            op0=ALU.mult, op1=ALU.add),
             wait=[s.s_sc], inc=[s.s_v])
        for g in (0, 1):
            bank = 6 + g
            ptb = self.pt[g]
            for f8 in range(8):
                fc = g * 8 + f8
                s.pe_tr(ptb[:, f8 * 128:(f8 + 1) * 128],
                        s.h_tile[:, fc * 128:(fc + 1) * 128], s.id_b[:],
                        bank=bank, wait=[s.s_v])
            eng = "vector" if g == 0 else "scalar"
            for f8 in range(8):
                fc = g * 8 + f8
                dsl = dstT[:, fc * dst_stride + tok_off: fc * dst_stride + tok_off + 128]
                psl = ptb[:, f8 * 128:(f8 + 1) * 128]
                if eng == "vector":
                    s.evac(eng, lambda e, d=dsl, ps=psl: e.tensor_copy(out=d, in_=ps),
                           bank=bank)
                else:
                    s.evac(eng, lambda e, d=dsl, ps=psl: e.copy(out=d, in_=ps), bank=bank)

    def emit_window_proj(self, w):
        p, s = self.p, self
        self.emit_mod_bcast("a")
        for t in range(NT):
            self.emit_ln_tile(s.xw[w, t * 128:(t + 1) * 128, :], s.hT, TP, t * 128,
                              last_readers=[s.s_v, s.s_sc])
        if self.debug:
            p.on("gpsimd", lambda e: e.dma_start(out=s.hT_dbg[w], in_=s.hT[:]),
                 wait=[s.s_v, s.s_sc], inc=[(s.s_gd, 16)])
            if w == 1:
                p.on("gpsimd", lambda e: e.dma_start(out=s.bc_dbg[0], in_=s.bcA[:]),
                     wait=[s.s_v, s.s_sc], inc=[(s.s_gd, 16)])
                p.on("gpsimd", lambda e: e.dma_start(out=s.bc_dbg[1], in_=s.bcB[:]),
                     wait=[s.s_v, s.s_sc], inc=[(s.s_gd, 16)])
        fcs = []
        if w == 1:
            fcs += [("q", i) for i in range(NFQ)]
        fcs += [("k", i) for i in range(NFK)]
        fcs += [("v", i) for i in range(NFV)]
        if w == 1:
            fcs += [("g", i) for i in range(NFG)]
        wsrc = {"q": s.wq, "k": s.wk, "v": s.wv, "g": s.wg}

        def stage_proj(idx):
            kind, fc = fcs[idx]
            half = idx % 2
            po = half * 2048
            p.on("sync", lambda e, kind=kind, fc=fc, po=po: e.dma_start(
                out=s.pan_f[:, po:po + 2048],
                in_=wsrc[kind][:, fc * 128:(fc + 1) * 128]
                .rearrange("(kc p) m -> p kc m", p=128)),
                 wait=[s.s_pe], inc=[(s.s_ld, 16)])
            p.on("vector", lambda e, po=po: e.tensor_copy(
                out=s.pan_b[:, po:po + 2048], in_=s.pan_f[:, po:po + 2048]),
                 wait=[s.s_ld, s.s_pe], inc=[s.s_w])
        if fcs:
            stage_proj(0)
            if len(fcs) > 1:
                stage_proj(1)
        for idx, (kind, fc) in enumerate(fcs):
            half = idx % 2
            po, pb = half * 2048, half * 2048
            span = TP if kind in ("q", "k", "v") else TW
            toff = 0 if kind in ("q", "k", "v") else 128
            nmt = 3 if span == TP else 2
            mw = span // nmt
            ph = half
            for mt in range(nmt):
                bank = mt % 2
                for kc in range(KC):
                    s.pe_mm(self.pbank[bank][:, 0:mw],
                            s.pan_b[:, pb + kc * 128:pb + (kc + 1) * 128],
                            s.hT[:, kc * TP + toff + mt * mw: kc * TP + toff + (mt + 1) * mw],
                            bank=bank, start=(kc == 0), stop=(kc == 15),
                            wait=([s.s_w] + ([s.s_v, s.s_sc] if idx == 0 else []))
                            if kc == 0 else [], inc_pe=(kc == 15))
                if kind in ("q", "k", "v"):
                    dsl = s.praw[:, ph * TP + mt * mw: ph * TP + (mt + 1) * mw]
                    s.evac("scalar", lambda e, d=dsl, bank=bank, mw=mw: e.copy(
                        out=d, in_=self.pbank[bank][:, 0:mw]), bank=bank,
                           wait=[s.s_v, s.s_gd])
                else:
                    dsl = s.cacc[:, ph * TW + mt * mw: ph * TW + (mt + 1) * mw]
                    s.evac("scalar", lambda e, d=dsl, bank=bank, mw=mw: e.activation(
                        out=d, in_=self.pbank[bank][:, 0:mw], func=AF.Silu), bank=bank,
                           wait=[s.s_v])
            if idx + 2 < len(fcs):
                stage_proj(idx + 2)
            if kind in ("q", "k", "v"):
                if w == 0:
                    p.on("scalar", lambda e, ph=ph: e.memzero(
                        s.praw[:, ph * TP + 124: ph * TP + 128]), inc=[s.s_sc])
                else:
                    p.on("scalar", lambda e, ph=ph: e.activation(
                        out=s.praw[:, ph * TP + 124: ph * TP + 128],
                        in_=s.praw[:, ph * TP + 124: ph * TP + 128],
                        func=AF.Copy, scale=s.mskhalo_t[:, 0:1]), inc=[s.s_sc])
                cw = s.convw_t
                cwbase = {"q": 0, "k": NFQ, "v": NFQ + NFK}[kind] + fc
                ca = s.cacc[:, ph * TW:(ph + 1) * TW]
                p.on("vector", lambda e, ph=ph, cwbase=cwbase, ca=ca: e.tensor_scalar_mul(
                    out=ca, in0=s.praw[:, ph * TP + 125: ph * TP + 125 + TW],
                    scalar1=cw[:, cwbase * 4:cwbase * 4 + 1]),
                     wait=[s.s_sc], inc=[s.s_v])
                for i in (1, 2, 3):
                    p.on("vector", lambda e, ph=ph, cwbase=cwbase, i=i, ca=ca:
                         e.scalar_tensor_tensor(
                             out=ca, in0=s.praw[:, ph * TP + 125 + i: ph * TP + 125 + i + TW],
                             scalar=cw[:, cwbase * 4 + i:cwbase * 4 + i + 1], in1=ca,
                             op0=ALU.mult, op1=ALU.add), inc=[s.s_v])
                if kind in ("q", "k"):
                    p.on("scalar", lambda e, ph=ph, ca=ca: e.activation(
                        out=s.praw[:, ph * TP:ph * TP + TW], in_=ca, func=AF.Silu),
                         wait=[s.s_v], inc=[s.s_sc])
                    dd = s.kf_d if kind == "k" else s.qf_d
                    p.on("gpsimd", lambda e, dd=dd, fc=fc, ph=ph: e.dma_start(
                        out=dd[fc], in_=s.praw[:, ph * TP:ph * TP + TW]),
                         wait=[s.s_sc], inc=[(s.s_gd, 16)])
                else:
                    p.on("scalar", lambda e, ph=ph, ca=ca: e.activation(
                        out=s.praw[:, ph * TP:ph * TP + TW], in_=ca, func=AF.Silu),
                         wait=[s.s_v], inc=[s.s_sc])
                    self._transpose_to_pairs(s.praw[:, ph * TP:ph * TP + TW],
                                             s.vtok_d, fc, wait=[s.s_sc])
            else:
                p.on("vector", lambda e, ph=ph: e.tensor_copy(
                    out=s.praw[:, ph * TP:ph * TP + TW], in_=s.cacc[:, ph * TW:(ph + 1) * TW]),
                     wait=[s.s_sc], inc=[s.s_v])
                self._transpose_to_pairs(s.praw[:, ph * TP:ph * TP + TW],
                                         s.gate_d, fc, wait=[s.s_v])
        for tt in range(NPAIR):
            bank = 4
            for kc in range(KC):
                s.pe_mm(self.pbank[bank][:, 0:16],
                        s.hT[:, kc * TP + 128 + tt * 128: kc * TP + 256 + tt * 128],
                        s.wab_b[:, kc * 16:(kc + 1) * 16],
                        bank=bank, start=(kc == 0), stop=(kc == 15),
                        wait=[s.s_v, s.s_sc] if kc == 0 else [], inc_pe=(kc == 15))
            s.evac("vector", lambda e, tt=tt: e.tensor_copy(
                out=s.abraw[:, tt * 16:(tt + 1) * 16], in_=self.pbank[bank][:, 0:16]),
                   bank=bank)
            ab = s.abraw[:, tt * 16:tt * 16 + 6]
            bb = s.abraw[:, tt * 16 + 8:tt * 16 + 14]
            p.on("scalar", lambda e, bb=bb, tt=tt: e.activation(
                out=s.bwin[:, tt * 8:tt * 8 + 6], in_=bb, func=AF.Sigmoid),
                 wait=[s.s_v], inc=[s.s_sc])
            p.on("vector", lambda e, tt=tt: e.tensor_scalar_mul(
                out=s.nbwin[:, tt * 8:tt * 8 + 6], in0=s.bwin[:, tt * 8:tt * 8 + 6],
                scalar1=-1.0), wait=[s.s_sc], inc=[s.s_v])
            p.on("vector", lambda e, ab=ab: e.tensor_add(
                out=s.smallp[:, 0:6], in0=ab, in1=s.dtb_t[:, 0:6]),
                 wait=[s.s_v], inc=[s.s_v])
            p.on("vector", lambda e: e.tensor_relu(out=s.smallp[:, 8:14],
                                                   in_=s.smallp[:, 0:6]),
                 wait=[s.s_v], inc=[s.s_v])
            p.on("scalar", lambda e: e.activation(out=s.smallp[:, 16:22],
                                                  in_=s.smallp[:, 0:6], func=AF.Abs),
                 wait=[s.s_v], inc=[s.s_sc])
            p.on("scalar", lambda e: e.activation(out=s.smallp[:, 16:22],
                                                  in_=s.smallp[:, 16:22], func=AF.Exp,
                                                  scale=-1.0), inc=[s.s_sc])
            p.on("scalar", lambda e: e.activation(out=s.smallp[:, 16:22],
                                                  in_=s.smallp[:, 16:22], func=AF.Ln,
                                                  bias=s.cpool[:, 2:3]), inc=[s.s_sc])
            p.on("vector", lambda e: e.tensor_add(out=s.smallp[:, 8:14],
                                                  in0=s.smallp[:, 8:14],
                                                  in1=s.smallp[:, 16:22]),
                 wait=[s.s_sc, s.s_v], inc=[s.s_v])
            p.on("vector", lambda e, tt=tt: e.tensor_mul(
                out=s.gwin[:, tt * 8:tt * 8 + 6], in0=s.smallp[:, 8:14],
                in1=s.negA_t[:, 0:6]), wait=[s.s_v], inc=[s.s_v])

    def _transpose_to_pairs(self, src_bf, dram_dst, fc, wait):
        p, s = self.p, self
        for g in (0, 1):
            bank = 6 + g
            ptb = self.pt[g]
            for f4 in range(4):
                pr = g * 4 + f4
                s.pe_tr(ptb[:, f4 * 128:(f4 + 1) * 128],
                        src_bf[:, pr * 128:(pr + 1) * 128], s.id_b[:],
                        bank=bank, wait=wait if (pr == 0) else [])
            eng = "vector" if g == 0 else "scalar"
            for f4 in range(4):
                pr = g * 4 + f4
                dsl = s.vasm[:, pr * 128:(pr + 1) * 128]
                psl = ptb[:, f4 * 128:(f4 + 1) * 128]
                war = [s.s_gd] if f4 == 0 else []
                if eng == "vector":
                    s.evac(eng, lambda e, d=dsl, ps=psl: e.tensor_copy(out=d, in_=ps),
                           bank=bank, wait=war)
                else:
                    s.evac(eng, lambda e, d=dsl, ps=psl: e.copy(out=d, in_=ps),
                           bank=bank, wait=war)
        p.on("gpsimd", lambda e: e.dma_start(
            out=dram_dst[:, :, fc * 128:(fc + 1) * 128].rearrange("pr p m -> p pr m"),
            in_=s.vasm[:].rearrange("p (pr m) -> p pr m", pr=8)),
             wait=[s.s_v, s.s_sc], inc=[(s.s_gd, 16)])

    def emit_scan(self, w):
        p, s = self.p, self
        full = (w == 1)
        if w == 0:
            p.on("vector", lambda e: e.memset(s.S_sb[:], 0.0), wait=[s.s_pe], inc=[s.s_v])
        else:
            p.on("vector", lambda e: e.tensor_scalar_mul(
                out=s.S_sb[:], in0=s.S_sb[:], scalar1=s.mskpre_t[:, 0:1]),
                 wait=[s.s_pe], inc=[s.s_v])
        npair, nhead = NPAIR, H
        if self.stop_after and self.stop_after.startswith(f"w{w}scan:"):
            parts = self.stop_after.split(":")
            npair = int(parts[1])
            nhead = int(parts[2]) if len(parts) > 2 else H
        for pr in range(npair):
            self.emit_scan_pair(w, pr, full,
                                nhead if pr == npair - 1 else H)

    def emit_scan_pair(self, w, pr, full, nhead=H):
        p, s = self.p, self
        sub = 99
        if nhead < 0:
            sub, nhead = -nhead, 0
        half = pr % 2
        kslot = s.kfp[:, half * NFK * 128:(half + 1) * NFK * 128]
        qslot = s.qfp[:, half * NFQ * 128:(half + 1) * NFQ * 128]
        vslot = s.vpair[:, half * 3072:(half + 1) * 3072]
        gslot = s.gatep[:, half * 3072:(half + 1) * 3072]
        war = [s.s_pe, s.s_v0, s.s_v1] if pr >= 2 else [s.s_gd]
        p.on("sync", lambda e: e.dma_start(
            out=kslot, in_=s.kf_d[:, :, pr * 128:(pr + 1) * 128]
            .rearrange("f p m -> p f m")), wait=war, inc=[(s.s_ld, 16)])
        p.on("sync", lambda e: e.dma_start(
            out=vslot, in_=s.vtok_d[pr].rearrange("p m -> p m")), wait=war,
             inc=[(s.s_ld, 16)])
        if full:
            p.on("sync", lambda e: e.dma_start(
                out=qslot, in_=s.qf_d[:, :, pr * 128:(pr + 1) * 128]
                .rearrange("f p m -> p f m")), wait=war, inc=[(s.s_ld, 16)])
            p.on("sync", lambda e: e.dma_start(
                out=gslot, in_=s.gate_d[pr].rearrange("p m -> p m")), wait=war,
                 inc=[(s.s_ld, 16)])
        ld_target = s.s_ld.n
        if sub < 2:
            return

        pc = pr * 8
        for c in (0, 1):
            s.pe_mm(self.pbank[0][c * 64:(c + 1) * 64, 0:6],
                    s.cum_u[c * 64:(c + 1) * 64, 0:64],
                    s.gwin[c * 64:(c + 1) * 64, pc:pc + 6],
                    bank=0, start=True, stop=True,
                    wait=[s.s_v] if c == 0 else [], inc_pe=(c == 1))
        s.evac("vector", lambda e: e.tensor_copy(out=s.gamp[:, 0:6],
                                                 in_=self.pbank[0][:, 0:6]), bank=0)
        s.evac("scalar", lambda e: e.activation(out=s.bp[:, 0:6],
                                                in_=self.pbank[0][:, 0:6], func=AF.Exp),
               bank=0, wait=[s.s_v])
        if sub < 3:
            return
        p.on("vector", lambda e: e.tensor_mul(out=s.nbb[:, 0:6],
                                              in0=s.nbwin[:, pc:pc + 6],
                                              in1=s.bp[:, 0:6]),
             wait=[s.s_v, s.s_sc], inc=[s.s_v])

        for h in range(nhead):
            self.emit_scan_head_A(w, pr, h, half, full, kslot, qslot, ld_target)
        for h in range(nhead):
            self.emit_scan_head_B(w, pr, h, half, full, kslot, qslot, vslot, gslot,
                                  ld_target)
        if full:
            p.on("gpsimd", lambda e: e.dma_start(out=s.o_d[pr], in_=s.oasm[:]),
                 wait=[s.s_v0, s.s_v1], inc=[(s.s_gd, 16)])

    def emit_scan_head_A(self, w, pr, h, half, full, kslot, qslot, ld_tgt):
        p, s = self.p, self
        pc = pr * 8
        par = h % 2
        vp = s.s_v0 if par == 0 else s.s_v1
        scp = s.s_c0 if par == 0 else s.s_c1
        g1 = [s.s_v, s.s_sc] if h < 2 else []
        ptb = self.pt[par]
        bkk = 0 if par == 0 else 2
        binv = 1 if par == 0 else 3
        ksl = [kslot[:, (2 * h + d) * 128:(2 * h + d + 1) * 128] for d in (0, 1)]
        ko = par * DK
        nq = s.nsq[:, h * 8:h * 8 + 8]
        Es = s.E_sb[:, par * 128:(par + 1) * 128]
        ax = s.AX1[:, par * 128:(par + 1) * 128]
        axt = s.AXT[:, par * 128:(par + 1) * 128]
        apt = s.APT6[:, h * 128:(h + 1) * 128]
        dqk = s.DqK[:, par * 128:(par + 1) * 128]
        mut = s.MuT6[:, h * 128:(h + 1) * 128]
        ke = s.ketok6[:, h * DK:(h + 1) * DK]
        sqs = s.sq_scr[:, par * 256:(par + 1) * 256]
        for d in (0, 1):
            s.pe_tr(ptb[:, d * 128:(d + 1) * 128], ksl[d], s.id_b[:], bank=6 + par,
                    wait=[s.s_ld] if h < 2 else [])
        for d in (0, 1):
            s.evac("vector", lambda e, d=d: e.tensor_copy(
                out=s.ktok[:, ko + d * 128:ko + (d + 1) * 128],
                in_=ptb[:, d * 128:(d + 1) * 128]), bank=6 + par, sem=vp)
        p.on("scalar", lambda e: e.activation(out=sqs, in_=s.ktok[:, ko:ko + 256],
                                              func=AF.Square, accum_out=nq[:, 0:1]),
             wait=[vp], inc=[scp])
        p.on("scalar", lambda e: e.activation(out=nq[:, 1:2], in_=nq[:, 0:1],
                                              func=AF.Ln), inc=[scp])
        p.on("scalar", lambda e: e.activation(out=nq[:, 2:3], in_=nq[:, 1:2],
                                              func=AF.Exp, scale=-0.5), inc=[scp])
        p.on("vector", lambda e: e.tensor_scalar_mul(out=s.ktok[:, ko:ko + 256],
                                                     in0=s.ktok[:, ko:ko + 256],
                                                     scalar1=nq[:, 2:3]),
             wait=[scp], inc=[vp])
        for d in (0, 1):
            s.pe_tr(ptb[:, (2 + d) * 128:(3 + d) * 128],
                    s.ktok[:, ko + d * 128:ko + (d + 1) * 128], s.id_b[:],
                    bank=6 + par, wait=[vp])
        for d in (0, 1):
            s.evac("vector", lambda e, d=d: e.tensor_copy(
                out=ksl[d], in_=ptb[:, (2 + d) * 128:(3 + d) * 128]), bank=6 + par, sem=vp)
        if full:
            qsl = [qslot[:, (2 * h + d) * 128:(2 * h + d + 1) * 128] for d in (0, 1)]
            for d in (0, 1):
                s.pe_tr(ptb[:, (4 + d) * 128:(5 + d) * 128], qsl[d], s.id_b[:],
                        bank=6 + par, wait=[s.s_ld] if h < 2 else [])
            for d in (0, 1):
                s.evac("vector", lambda e, d=d: e.tensor_copy(
                    out=sqs[:, d * 128:(d + 1) * 128],
                    in_=ptb[:, (4 + d) * 128:(5 + d) * 128]), bank=6 + par, sem=vp)
            p.on("scalar", lambda e: e.activation(out=sqs, in_=sqs, func=AF.Square,
                                                  accum_out=nq[:, 4:5]),
                 wait=[vp], inc=[scp])
            p.on("scalar", lambda e: e.activation(out=nq[:, 5:6], in_=nq[:, 4:5],
                                                  func=AF.Ln), inc=[scp])
            p.on("scalar", lambda e: e.activation(out=nq[:, 6:7], in_=nq[:, 5:6],
                                                  func=AF.Exp, scale=-0.5,
                                                  bias=s.cpool[:, 1:2]), inc=[scp])
            p.on("vector", lambda e: e.tensor_mul(out=nq[:, 7:8],
                                                  in0=s.bp[:, h:h + 1],
                                                  in1=nq[:, 6:7]),
                 wait=[scp] + g1, inc=[vp])
        for d in (0, 1):
            s.pe_mm(self.pbank[bkk][:, 0:128], ksl[d], ksl[d],
                    bank=bkk, start=(d == 0), stop=(d == 1),
                    wait=[vp] if d == 0 else [], inc_pe=(not full) and d == 1)
        if full:
            qsl = [qslot[:, (2 * h + d) * 128:(2 * h + d + 1) * 128] for d in (0, 1)]
            for d in (0, 1):
                s.pe_mm(self.pbank[bkk][:, 128:256], qsl[d], ksl[d],
                        bank=bkk, start=(d == 0), stop=(d == 1), inc_pe=(d == 1))
        p.on("vector", lambda e: e.tensor_scalar_mul(
            out=s.sq_scr[:, par * 256:par * 256 + 128], in0=s.ones128[:, 0:128],
            scalar1=s.gamp[:, h:h + 1]),
             wait=[vp, scp, s.s_v], inc=[vp])
        p.on("tensor", lambda e: e.transpose(
            out=self.pbank[binv][:, 0:128], in_=s.sq_scr[:, par * 256:par * 256 + 128],
            identity=s.id_f[:]),
             wait=[vp], wait_at=[self.bank_last[binv]], inc=[self.s_pe])
        s.evac("vector", lambda e: e.tensor_copy(
            out=s.lamg[:, 2 * h:2 * h + 1], in_=self.pbank[binv][:, 63:64]), bank=binv, sem=vp)
        s.evac("vector", lambda e: e.tensor_copy(
            out=s.lamg[:, 2 * h + 1:2 * h + 2], in_=self.pbank[binv][:, 127:128]),
               bank=binv, sem=vp)
        p.on("scalar", lambda e: e.activation(out=s.lamt[:, 2 * h:2 * h + 2],
                                              in_=s.lamg[:, 2 * h:2 * h + 2],
                                              func=AF.Exp), wait=[vp], inc=[scp])
        p.on("vector", lambda e: e.tensor_sub(
            out=s.eep[0:64, h:h + 1], in0=s.lamg[0:64, 2 * h:2 * h + 1],
            in1=s.gamp[0:64, h:h + 1]), wait=[vp], inc=[vp])
        p.on("vector", lambda e: e.tensor_sub(
            out=s.eep[64:128, h:h + 1], in0=s.lamg[64:128, 2 * h + 1:2 * h + 2],
            in1=s.gamp[64:128, h:h + 1]), inc=[vp])
        p.on("scalar", lambda e: e.activation(out=s.eep[:, h:h + 1],
                                              in_=s.eep[:, h:h + 1], func=AF.Exp),
             wait=[vp], inc=[scp])
        p.on("vector", lambda e: e.tensor_scalar_mul(out=ke, in0=s.ktok[:, ko:ko + 256],
                                                     scalar1=s.eep[:, h:h + 1]),
             wait=[scp], inc=[vp])
        for c in (0, 1):
            p.on("vector", lambda e, c=c: e.tensor_scalar_mul(
                out=s.lamI[:, h * 256 + c * 128:h * 256 + (c + 1) * 128],
                in0=s.id_f[:],
                scalar1=s.lamt[:, 2 * h + c:2 * h + c + 1]),
                 wait=[scp], inc=[vp])
        s.evac("vector", lambda e: e.tensor_scalar(
            out=Es, in0=self.pbank[binv][:, 0:128], scalar1=s.gamp[:, h:h + 1],
            scalar2=0.0, op0=ALU.subtract, op1=ALU.max), bank=binv, wait=[vp], sem=vp)
        p.on("scalar", lambda e: e.activation(out=Es, in_=Es, func=AF.Exp,
                                              scale=-1.0), wait=[vp], inc=[scp])
        s.evac("vector", lambda e: e.tensor_mul(out=sqs[:, 0:128], in0=Es,
                                                in1=self.pbank[bkk][:, 0:128]),
               bank=bkk, wait=[scp], sem=vp)
        p.on("vector", lambda e: e.tensor_mul(out=sqs[:, 0:128],
                                              in0=sqs[:, 0:128], in1=s.mskS[:]),
             inc=[vp])
        p.on("vector", lambda e: e.tensor_scalar_mul(out=ax, in0=sqs[:, 0:128],
                                                     scalar1=s.nbwin[:, pc + h:pc + h + 1]),
             inc=[vp])
        if full:
            s.evac("vector", lambda e: e.scalar_tensor_tensor(
                out=sqs[:, 128:256], in0=self.pbank[bkk][:, 128:256],
                scalar=nq[:, 6:7], in1=Es, op0=ALU.mult, op1=ALU.mult),
                   bank=bkk, wait=[scp], sem=vp)
            p.on("vector", lambda e: e.tensor_mul(out=dqk, in0=sqs[:, 128:256],
                                                  in1=s.mskI[:]), inc=[vp])
            s.pe_tr(ptb[:, 6 * 128:7 * 128], dqk, s.id_b[:], bank=6 + par,
                    wait=[vp])
            s.evac("vector", lambda e: e.tensor_copy(out=mut,
                                                     in_=ptb[:, 6 * 128:7 * 128]),
                   bank=6 + par)
        s.pe_tr(ptb[:, 7 * 128:8 * 128], ax, s.id_b[:], bank=6 + par, wait=[vp])
        s.evac("vector", lambda e: e.tensor_copy(out=axt, in_=ptb[:, 7 * 128:8 * 128]),
               bank=6 + par)
        p.on("vector", lambda e: e.tensor_add(out=apt, in0=axt, in1=s.id_b[:]),
             wait=[vp], inc=[vp])
        for lvl in range(1, 6):
            s.pe_mm(self.pbank[binv][:, 128:256], axt, ax, bank=binv,
                    start=True, stop=True, wait=[vp])
            s.evac("vector", lambda e: e.tensor_copy(out=ax,
                                                     in_=self.pbank[binv][:, 128:256]),
                   bank=binv, sem=vp)
            s.pe_mm(self.pbank[binv][:, 256:384], ax, apt, bank=binv,
                    start=True, stop=True, wait=[vp])
            if lvl < 5:
                s.pe_tr(ptb[:, 7 * 128:8 * 128], ax, s.id_b[:], bank=6 + par,
                        wait=[vp])
                s.evac("vector", lambda e: e.tensor_copy(
                    out=axt, in_=ptb[:, 7 * 128:8 * 128]), bank=6 + par, sem=vp)
            s.evac("vector", lambda e: e.tensor_add(out=apt,
                                                    in0=self.pbank[binv][:, 256:384],
                                                    in1=apt), bank=binv, sem=vp)

    def emit_scan_head_B(self, w, pr, h, half, full, kslot, qslot, vslot, gslot,
                         ld_tgt):
        p, s = self.p, self
        pc = pr * 8
        par = h % 2
        vp = s.s_v0 if par == 0 else s.s_v1
        scp = s.s_c0 if par == 0 else s.s_c1
        g1 = [s.s_v, s.s_sc] if h < 2 else []
        bst = 4 if par == 0 else 1
        bs1 = 5 if par == 0 else 2
        bqm = 0 if par == 0 else 3
        ksl = [kslot[:, (2 * h + d) * 128:(2 * h + d + 1) * 128] for d in (0, 1)]
        qsl = [qslot[:, (2 * h + d) * 128:(2 * h + d + 1) * 128] for d in (0, 1)]
        nq = s.nsq[:, h * 8:h * 8 + 8]
        apt = s.APT6[:, h * 128:(h + 1) * 128]
        mut = s.MuT6[:, h * 128:(h + 1) * 128]
        ke = s.ketok6[:, h * DK:(h + 1) * DK]
        vo, wo_, uo, to = par * DV, par * DV, par * DV, par * DV
        bVs = s.bV[:, vo:vo + DV]
        wsb = s.w_sb[:, wo_:wo_ + DV]
        ubf = s.u_bf[:, uo:uo + DV]
        t2s = s.t2[:, to:to + DV]
        Sd = [s.S_sb[:, (h * 2 + d) * DV:(h * 2 + d + 1) * DV] for d in (0, 1)]
        p.on("vector", lambda e: e.tensor_scalar(
            out=bVs, in0=vslot[:, h * DV:(h + 1) * DV],
            scalar1=s.bwin[:, pc + h:pc + h + 1], scalar2=None, op0=ALU.mult),
             wait=([s.s_ld] if h < 2 else []) + [scp] + g1, inc=[vp])
        for c in (0, 1):
            rows = slice(c * 64, (c + 1) * 64)
            for d in (0, 1):
                s.pe_mm(self.pbank[bst][rows, 0:512], ksl[d][:, rows], Sd[d],
                        bank=bst, start=(d == 0), stop=(d == 1),
                        wait=[vp, scp], inc_pe=(d == 1))
            s.evac("vector", lambda e, rows=rows: e.scalar_tensor_tensor(
                out=wsb[rows, :], in0=self.pbank[bst][rows, 0:512],
                scalar=s.nbb[rows, h:h + 1], in1=bVs[rows, :],
                op0=ALU.mult, op1=ALU.add), bank=bst, sem=vp)
            s.pe_mm(self.pbank[bst][rows, 0:512],
                    apt[c * 64:(c + 1) * 64, c * 64:(c + 1) * 64],
                    wsb[rows, :], bank=bst, start=True, stop=True, wait=[vp])
            s.evac("vector", lambda e, rows=rows: e.tensor_copy(
                out=ubf[rows, :], in_=self.pbank[bst][rows, 0:512]), bank=bst, sem=vp)
            if full:
                for d in (0, 1):
                    s.pe_mm(self.pbank[bqm][rows, 0:512], qsl[d][:, rows], Sd[d],
                            bank=bqm, start=(d == 0), stop=(d == 1),
                            wait=[vp, scp] if d == 0 else [], inc_pe=(d == 1))
                s.evac("scalar", lambda e, rows=rows: e.activation(
                    out=t2s[rows, :], in_=self.pbank[bqm][rows, 0:512], func=AF.Copy,
                    scale=nq[rows, 7:8]), bank=bqm, wait=[vp], sem=scp)
                s.pe_mm(self.pbank[bqm][rows, 0:512],
                        mut[c * 64:(c + 1) * 64, c * 64:(c + 1) * 64],
                        ubf[rows, :], bank=bqm, start=True, stop=True,
                        wait=[vp, scp])
            for d in (0, 1):
                bank = bst if d == 0 else bs1
                s.pe_mm(self.pbank[bank][:, 0:512],
                        s.lamI[:, h * 256 + c * 128:h * 256 + (c + 1) * 128],
                        Sd[d], bank=bank, start=True, stop=False,
                        wait=[vp, scp] + g1)
                s.pe_mm(self.pbank[bank][:, 0:512],
                        ke[rows, d * 128:(d + 1) * 128], ubf[rows, :],
                        bank=bank, start=False, stop=True, inc_pe=True)
            for d in (0, 1):
                bank = bst if d == 0 else bs1
                eng = "vector" if d == 0 else "scalar"
                if eng == "vector":
                    s.evac(eng, lambda e, d=d, bank=bank: e.tensor_copy(
                        out=Sd[d], in_=self.pbank[bank][:, 0:512]), bank=bank, sem=vp)
                else:
                    s.evac(eng, lambda e, d=d, bank=bank: e.copy(
                        out=Sd[d], in_=self.pbank[bank][:, 0:512]), bank=bank, sem=scp)
            if full and c == 1:
                s.evac("vector", lambda e: e.tensor_add(out=t2s, in0=t2s,
                                                        in1=self.pbank[bqm][:, 0:512]),
                       bank=bqm, wait=[scp], sem=vp)
        if full:
            p.on("scalar", lambda e: e.activation(out=bVs, in_=t2s,
                                                  func=AF.Square,
                                                  accum_out=nq[:, 3:4]),
                 wait=[vp], inc=[scp])
            p.on("scalar", lambda e: e.activation(out=nq[:, 3:4], in_=nq[:, 3:4],
                                                  func=AF.Ln, scale=1.0 / DV,
                                                  bias=s.cpool[:, 0:1]), inc=[scp])
            p.on("scalar", lambda e: e.activation(out=nq[:, 3:4], in_=nq[:, 3:4],
                                                  func=AF.Exp, scale=-0.5), inc=[scp])
            p.on("vector", lambda e: e.scalar_tensor_tensor(
                out=t2s, in0=t2s, scalar=nq[:, 3:4], in1=s.normw_t[:],
                op0=ALU.mult, op1=ALU.mult), wait=[scp], inc=[vp])
            p.on("vector", lambda e: e.tensor_mul(
                out=s.oasm[:, h * DV:(h + 1) * DV], in0=t2s,
                in1=gslot[:, h * DV:(h + 1) * DV]),
                 wait=[s.s_gd] if (h == 0) else [], inc=[vp])

    def emit_phaseD(self):
        p, s = self.p, self
        self.emit_mod_bcast("ga")
        for tt in range(NPAIR):
            p.on("sync", lambda e, tt=tt: e.dma_start(out=s.otile[:], in_=s.o_d[tt]),
                 wait=[s.s_pe] if tt > 0 else [], inc=[(s.s_ld, 16)])
            for g in range(3):
                bank = 6 + (g % 2)
                ptb = self.pt[g % 2]
                for f8 in range(8):
                    fc = g * 8 + f8
                    s.pe_tr(ptb[:, f8 * 128:(f8 + 1) * 128],
                            s.otile[:, fc * 128:(fc + 1) * 128], s.id_b[:],
                            bank=bank, wait=[s.s_ld] if (g == 0 and f8 == 0) else [])
                eng = "vector" if (g % 2 == 0) else "scalar"
                for f8 in range(8):
                    fc = g * 8 + f8
                    dsl = s.oT[:, fc * TW + tt * 128: fc * TW + (tt + 1) * 128]
                    psl = ptb[:, f8 * 128:(f8 + 1) * 128]
                    if eng == "vector":
                        s.evac(eng, lambda e, d=dsl, ps=psl: e.tensor_copy(out=d, in_=ps),
                               bank=bank)
                    else:
                        s.evac(eng, lambda e, d=dsl, ps=psl: e.copy(out=d, in_=ps),
                               bank=bank)
        for dt in range(4):
            for kcv in range(24):
                p.on("sync", lambda e, dt=dt, kcv=kcv: e.dma_start(
                    out=s.pan_f[:, (kcv % 2) * 2048:(kcv % 2) * 2048 + 512],
                    in_=s.wo[kcv * 128:(kcv + 1) * 128, dt * 512:(dt + 1) * 512]),
                     wait=[s.s_w], inc=[(s.s_ld, 16)])
                p.on("vector", lambda e, kcv=kcv: e.tensor_copy(
                    out=s.wopan[:, kcv * 512:(kcv + 1) * 512],
                    in_=s.pan_f[:, (kcv % 2) * 2048:(kcv % 2) * 2048 + 512]),
                     wait=[s.s_ld, s.s_pe], inc=[s.s_w])
            for tg in range(2):
                for tt4 in range(4):
                    tt = tg * 4 + tt4
                    bank = tt4
                    for kcv in range(24):
                        s.pe_mm(self.pbank[bank][:, 0:512],
                                s.oT[:, kcv * TW + tt * 128: kcv * TW + (tt + 1) * 128],
                                s.wopan[:, kcv * 512:(kcv + 1) * 512],
                                bank=bank, start=(kcv == 0), stop=(kcv == 23),
                                wait=([s.s_w] + ([s.s_v, s.s_sc] if (dt == 0 and tg == 0 and tt4 == 0) else []))
                                if kcv == 0 else [],
                                inc_pe=(kcv == 23))
                for tt4 in range(4):
                    tt = tg * 4 + tt4
                    bank = tt4
                    p.on("sync", lambda e, tt=tt, dt=dt: e.dma_start(
                        out=s.res_t[:, 0:512],
                        in_=s.xw[1, 128 + tt * 128:256 + tt * 128, dt * 512:(dt + 1) * 512]),
                         wait=[s.s_v], inc=[(s.s_ld, 16)])
                    s.evac("vector", lambda e, bank=bank, dt=dt: e.tensor_mul(
                        out=s.res_t[:, 512:1024], in0=self.pbank[bank][:, 0:512],
                        in1=s.bcA[:, dt * 512:(dt + 1) * 512]), bank=bank,
                           wait=[s.s_ld, s.s_gd])
                    p.on("vector", lambda e: e.tensor_add(
                        out=s.res_t[:, 512:1024], in0=s.res_t[:, 512:1024],
                        in1=s.res_t[:, 0:512]), inc=[s.s_v])
                    p.on("gpsimd", lambda e, tt=tt, dt=dt: e.dma_start(
                        out=s.x1_d[tt, :, dt * 512:(dt + 1) * 512],
                        in_=s.res_t[:, 512:1024]), wait=[s.s_v], inc=[(s.s_gd, 16)])
        self.emit_mod_bcast("b")
        for tt in range(NPAIR):
            self.emit_ln_tile(s.x1_d[tt], s.h2T, TW, tt * 128,
                              last_readers=[s.s_v, s.s_sc, s.s_gd])

    def emit_ffnA(self):
        p, s = self.p, self

        def stage_ffn(hc):
            po = (hc % 2) * 2048
            p.on("sync", lambda e, hc=hc, po=po: e.dma_start(
                out=s.pan_f[:, po:po + 2048],
                in_=s.wgt[:, hc * 128:(hc + 1) * 128]
                .rearrange("(kc p) m -> p kc m", p=128)),
                 wait=[s.s_pe], inc=[(s.s_ld, 16)])
            p.on("vector", lambda e, po=po: e.tensor_copy(
                out=s.pan_b[:, po:po + 2048], in_=s.pan_f[:, po:po + 2048]),
                 wait=[s.s_ld, s.s_pe], inc=[s.s_w])
            p.on("sync", lambda e, hc=hc, po=po: e.dma_start(
                out=s.fpan_f[:, po:po + 2048],
                in_=s.wut[:, hc * 128:(hc + 1) * 128]
                .rearrange("(kc p) m -> p kc m", p=128)),
                 wait=[s.s_pe], inc=[(s.s_ld, 16)])
            p.on("vector", lambda e, po=po: e.tensor_copy(
                out=s.fpan_b[:, po:po + 2048], in_=s.fpan_f[:, po:po + 2048]),
                 wait=[s.s_ld, s.s_pe], inc=[s.s_w])
        stage_ffn(0)
        stage_ffn(1)
        for hc in range(64):
            po = (hc % 2) * 2048
            gb0 = 0 if hc % 2 == 0 else 2
            gsl = s.gu_sb[:, 0:1024] if hc % 2 == 0 else s.gu_sb[:, 1024:2048]
            for mt in (0, 1):
                for kc in range(KC):
                    s.pe_mm(self.pbank[gb0 + mt][:, 0:512],
                            s.pan_b[:, po + kc * 128:po + (kc + 1) * 128],
                            s.h2T[:, kc * TW + mt * 512: kc * TW + (mt + 1) * 512],
                            bank=gb0 + mt, start=(kc == 0), stop=(kc == 15),
                            wait=([s.s_w] + ([s.s_v, s.s_sc] if hc == 0 else []))
                            if kc == 0 else [], inc_pe=(kc == 15))
            for mt in (0, 1):
                for kc in range(KC):
                    s.pe_mm(self.pbank[4 + mt][:, 0:512],
                            s.fpan_b[:, po + kc * 128:po + (kc + 1) * 128],
                            s.h2T[:, kc * TW + mt * 512: kc * TW + (mt + 1) * 512],
                            bank=4 + mt, start=(kc == 0), stop=(kc == 15),
                            wait=[s.s_w] if kc == 0 else [], inc_pe=(kc == 15))
            if hc + 2 < 64:
                stage_ffn(hc + 2)
            for mt in (0, 1):
                s.evac("scalar", lambda e, mt=mt, gb0=gb0, gsl=gsl: e.activation(
                    out=gsl[:, mt * 512:(mt + 1) * 512],
                    in_=self.pbank[gb0 + mt][:, 0:512], func=AF.Silu), bank=gb0 + mt,
                       wait=[s.s_gd])
            for mt in (0, 1):
                s.evac("vector", lambda e, mt=mt, gsl=gsl: e.tensor_mul(
                    out=gsl[:, mt * 512:(mt + 1) * 512],
                    in0=gsl[:, mt * 512:(mt + 1) * 512],
                    in1=self.pbank[4 + mt][:, 0:512]), bank=4 + mt, wait=[s.s_sc])
            p.on("gpsimd", lambda e, hc=hc, gsl=gsl: e.dma_start(
                out=s.gu_d[hc * 128:(hc + 1) * 128, :], in_=gsl),
                 wait=[s.s_v], inc=[(s.s_gd, 16)])

    def emit_ffnB(self):
        p, s = self.p, self
        self.emit_mod_bcast("gb", scr=self.res2_t)
        for dt in range(4):
            for cb in range(8):
                for hh in (0, 1):
                    p.on("sync", lambda e, dt=dt, cb=cb, hh=hh: e.dma_start(
                        out=s.pan_f[:, hh * 2048:(hh + 1) * 2048]
                            .rearrange("p (kc m) -> p kc m", m=64),
                        in_=s.wdt[hh * 4096:(hh + 1) * 4096,
                                  dt * 512 + cb * 64: dt * 512 + (cb + 1) * 64]
                        .rearrange("(kc p) m -> p kc m", p=128)),
                         wait=[s.s_w], inc=[(s.s_ld, 16)])
                    p.on("vector", lambda e, cb=cb, hh=hh: e.tensor_copy(
                        out=s.wdpan[:]
                        .rearrange("p (kc m) -> p kc m", m=512)
                        [:, hh * 32:(hh + 1) * 32, cb * 64:(cb + 1) * 64],
                        in_=s.pan_f[:, hh * 2048:(hh + 1) * 2048]
                        .rearrange("p (kc m) -> p kc m", m=64)),
                         wait=[s.s_ld, s.s_pe], inc=[s.s_w])
            for tt in range(NPAIR):
                gslot = s.gucol[:, (tt % 2) * 8192:(tt % 2 + 1) * 8192]
                p.on("sync", lambda e, tt=tt, gslot=gslot: e.dma_start(
                    out=gslot, in_=s.gu_d[:, tt * 128:(tt + 1) * 128]
                    .rearrange("(kc p) m -> p kc m", p=128)),
                     wait=[s.s_pe] if tt >= 2 else ([s.s_gd] if dt == 0 else []),
                     inc=[(s.s_ld, 16)])
                bank = 4 + (tt % 2)
                for kc in range(64):
                    s.pe_mm(self.pbank[bank][:, 0:512],
                            gslot[:, kc * 128:(kc + 1) * 128],
                            s.wdpan[:, kc * 512:(kc + 1) * 512],
                            bank=bank, start=(kc == 0), stop=(kc == 63),
                            wait=[s.s_w, s.s_ld] if kc == 0 else [],
                            inc_pe=(kc == 63))
                ro = (tt % 2) * 1024
                p.on("sync", lambda e, tt=tt, dt=dt, ro=ro: e.dma_start(
                    out=s.res2_t[:, ro:ro + 512],
                    in_=s.x1_d[tt, :, dt * 512:(dt + 1) * 512]),
                     wait=[s.s_v], inc=[(s.s_ld, 16)])
                s.evac("vector", lambda e, bank=bank, dt=dt, ro=ro: e.tensor_mul(
                    out=s.res2_t[:, ro + 512:ro + 1024], in0=self.pbank[bank][:, 0:512],
                    in1=s.bcA[:, dt * 512:(dt + 1) * 512]), bank=bank,
                       wait=[s.s_ld, s.s_gd])
                p.on("vector", lambda e, ro=ro: e.tensor_add(
                    out=s.res2_t[:, ro + 512:ro + 1024], in0=s.res2_t[:, ro + 512:ro + 1024],
                    in1=s.res2_t[:, ro:ro + 512]), inc=[s.s_v])
                p.on("gpsimd", lambda e, tt=tt, dt=dt, ro=ro: e.dma_start(
                    out=s.out[tt * 128:(tt + 1) * 128, dt * 512:(dt + 1) * 512],
                    in_=s.res2_t[:, ro + 512:ro + 1024]), wait=[s.s_v], inc=[(s.s_gd, 16)])

    def emit_final(self):
        self.p.on("gpsimd", None, wait=[self.s_gd])
        self.p.on("sync", None, wait=[self.s_gd])


def stage_inputs(inputs):
    x = inputs["L0_attn_hidden_states"]
    ident = np.eye(128, dtype=np.float32)
    identb = bfarr(np.eye(128))
    u64 = np.triu(np.ones((64, 64), np.float32))
    cumu = np.concatenate([u64, u64], 0)
    ms = np.zeros((128, 128), np.float32)
    mi = np.zeros((128, 128), np.float32)
    ms[0:64, 0:64] = np.tril(np.ones((64, 64)), -1)
    ms[64:128, 64:128] = np.tril(np.ones((64, 64)), -1)
    mi[0:64, 0:64] = np.tril(np.ones((64, 64)), 0)
    mi[64:128, 64:128] = np.tril(np.ones((64, 64)), 0)
    onesf = np.ones((1, 128), np.float32)
    constp = np.zeros((128, 8), np.float32)
    constp[:, 0] = EPS
    constp[:, 1] = np.log(DK ** -0.5)
    constp[:, 2] = 1.0
    constp[:, 3] = 0.25
    convw = np.zeros((128, 192), np.float32)
    cw = np.concatenate([inputs["L0_attn_conv_wq"], inputs["L0_attn_conv_wk"],
                         inputs["L0_attn_conv_wv"]], 0)
    for fc in range(48):
        convw[:, fc * 4:(fc + 1) * 4] = cw[fc * 128:(fc + 1) * 128, :]
    alog = np.zeros((1, 8), np.float32); alog[0, :6] = inputs["L0_attn_A_log"]
    dtb = np.zeros((1, 8), np.float32); dtb[0, :6] = inputs["L0_attn_dt_bias"]
    wab = np.zeros((D, 16), np.float32)
    wab[:, 0:6] = inputs["L0_attn_Wa"]
    wab[:, 8:14] = inputs["L0_attn_Wb"]
    shared = dict(
        wq=np.ascontiguousarray(inputs["L0_attn_Wq"]),
        wk=np.ascontiguousarray(inputs["L0_attn_Wk"]),
        wv=np.ascontiguousarray(inputs["L0_attn_Wv"]),
        wab=wab,
        wg=np.ascontiguousarray(inputs["L0_attn_Wg"]),
        wo=np.ascontiguousarray(inputs["L0_attn_Wo"]),
        wmod=np.ascontiguousarray(inputs["L0_wmod"]),
        wgt=np.ascontiguousarray(inputs["L0_ffn_w_gate"].T),
        wut=np.ascontiguousarray(inputs["L0_ffn_w_up"].T),
        wdt=np.ascontiguousarray(inputs["L0_ffn_w_down"].T),
        convw=convw, alog=alog, dtbv=dtb,
        normw=np.ascontiguousarray(inputs["L0_attn_norm_weight"].reshape(1, DV)),
        ident=ident, identb=identb, cumu=cumu, masks_c=ms, maski_c=mi,
        onesf=onesf, constp=constp,
    )
    z128 = np.zeros((128, D), np.float32)
    maps = []
    for core in range(8):
        b, th = core // 2, core % 2
        xb = x[b]
        xpre = np.concatenate([z128, xb[0:1024]], 0)
        base = th * 1024
        halo = z128 if th == 0 else xb[base - 128:base]
        xmain = np.concatenate([halo, xb[base:base + 1024]], 0)
        m = dict(shared)
        m["xw"] = np.ascontiguousarray(np.stack([xpre, xmain], 0))
        m["c_r"] = np.ascontiguousarray(inputs["L0_c"][b].reshape(16, 128))
        m["maskpre"] = np.full((128, 1), float(th == 1), np.float32)
        m["maskhalo"] = np.full((128, 1), float(th == 1), np.float32)
        maps.append(m)
    return maps


_CACHE = {}


def kernel(**inputs):
    if "nc" not in _CACHE:
        _CACHE["nc"] = KB().build()
    nc = _CACHE["nc"]
    maps = stage_inputs(inputs)
    res = run_bass_kernel_spmd(nc, maps, core_ids=list(range(8)))
    out = np.empty((B, T, D), np.float32)
    for core in range(8):
        b, th = core // 2, core % 2
        out[b, th * 1024:(th + 1) * 1024] = res.results[core]["out"]
    return out


# version 10
# speedup vs baseline: 1.0156x; 1.0156x over previous
import numpy as np
import ml_dtypes
import concourse.bass as bass
import concourse.mybir as mybir
from concourse.bass_utils import run_bass_kernel_spmd
from contextlib import ExitStack

F32 = mybir.dt.float32
BF16 = mybir.dt.bfloat16
AF = mybir.ActivationFunctionType
ALU = mybir.AluOpType
AX = mybir.AxisListType

B, T, D = 4, 2048, 2048
H, DK, DV = 6, 256, 512
TW = 1024
TP = TW + 128
NT = TP // 128
NPAIR = TW // 128
KC = D // 128
NFQ, NFK, NFV, NFG = 12, 12, 24, 24
DHID = 8192
EPS = 1e-5

bfarr = lambda x: np.ascontiguousarray(np.asarray(x)).astype(ml_dtypes.bfloat16)


class Sem:
    __slots__ = ("h", "n", "name")

    def __init__(self, ctx, nc, name):
        self.h = ctx.enter_context(nc.semaphore(name))
        self.n = 0
        self.name = name


class Plan:

    def __init__(self):
        self.ops = []

    def on(self, eng, fn, wait=(), inc=(), wait_at=()):
        waits = [(s, s.n) for s in wait if s.n > 0]
        waits += [(s, v) for (s, v) in wait_at if v > 0]
        incs = []
        for item in inc:
            s, k = item if isinstance(item, tuple) else (item, 1)
            s.n += k
            incs.append((s, k))
        self.ops.append((eng, fn, waits, incs))

    def realize(self, nc):
        ops = self.ops
        self.ops = []
        with nc.Block() as blk:
            def run_for(name):
                def body(e):
                    seen = {}
                    selfcnt = {}
                    for eng, fn, waits, incs in ops:
                        if eng != name:
                            continue
                        for s, v in waits:
                            if seen.get(s, -1) >= v:
                                continue
                            if selfcnt.get(s, 0) >= v:
                                pass
                            e.wait_ge(s.h, v)
                            seen[s] = v
                        if fn is not None:
                            ins = fn(e)
                            for s, k in incs:
                                ins.then_inc(s.h, k)
                                selfcnt[s] = selfcnt.get(s, 0) + k
                return body
            blk.sync(run_for("sync"))
            blk.tensor(run_for("tensor"))
            blk.vector(run_for("vector"))
            blk.scalar(run_for("scalar"))
            blk.gpsimd(run_for("gpsimd"))


class KB:
    def __init__(self, debug=False, stop_after=None):
        self.nc = bass.Bass("TRN2", target_bir_lowering=False)
        self.ctx = ExitStack()
        self.p = Plan()
        self.debug = debug
        self.stop_after = stop_after

    _name_ctr = {}

    def sb(self, scope, name, shape, dtype, side=None):
        n = KB._name_ctr.get(name, 0)
        KB._name_ctr[name] = n + 1
        uname = name if n == 0 else f"{name}_{n}"
        kw = {"side": side} if side else {}
        t = scope.enter_context(self.nc.sbuf_tensor(uname, shape, dtype, **kw))
        setattr(self, name, t)
        return t

    def sem(self, name):
        s = Sem(self.ctx, self.nc, name)
        setattr(self, name, s)
        return s

    def din(self, name, shape, dtype=F32):
        t = self.nc.dram_tensor(name, list(shape), dtype, kind="ExternalInput").ap()
        setattr(self, name, t)
        return t

    def dout(self, name, shape, dtype=F32):
        t = self.nc.dram_tensor(name, list(shape), dtype, kind="ExternalOutput").ap()
        setattr(self, name, t)
        return t

    def dint(self, name, shape, dtype=F32):
        kind = "ExternalOutput" if self.debug else "Internal"
        t = self.nc.dram_tensor(name, list(shape), dtype, kind=kind).ap()
        setattr(self, name, t)
        return t

    def pe_mm(self, out, lhsT, rhs, bank, start, stop, wait=(), inc_pe=True):
        wa = [self.bank_last[bank]] if start else []
        self.p.on("tensor",
                  lambda e: e.matmul(out=out, lhsT=lhsT, rhs=rhs, start=start, stop=stop),
                  wait=wait, wait_at=wa, inc=([self.s_pe] if (inc_pe and stop) else ()))

    def pe_tr(self, out, in_, ident, bank, wait=()):
        self.p.on("tensor", lambda e: e.transpose(out=out, in_=in_, identity=ident),
                  wait=wait, wait_at=[self.bank_last[bank]], inc=[self.s_pe])

    def evac(self, eng, fn, bank, wait=(), inc=(), sem=None):
        if sem is None:
            sem = self.s_v if eng == "vector" else self.s_sc
        self.p.on(eng, fn, wait=[self.s_pe] + list(wait), inc=[sem] + list(inc))
        self.bank_last[bank] = (sem, sem.n)

    def fence(self, engines=("tensor", "vector", "scalar", "gpsimd", "sync")):
        allsems = [self.s_pe, self.s_v, self.s_sc, self.s_gd, self.s_ld,
                   self.s_w, self.s_v0, self.s_v1, self.s_c0, self.s_c1]
        for e in engines:
            self.p.on(e, None, wait=allsems)

    def dbg_dump(self, ap):
        if not self.debug or self._dbg_i >= 32:
            return
        i = self._dbg_i; self._dbg_i += 1
        s = self
        self.p.on("vector", lambda e, ap=ap, i=i: e.tensor_copy(
            out=s.ln_scr[:, 0:ap.shape[-1]] if ap.shape[-1] <= D else None, in_=ap),
                  wait=[s.s_pe, s.s_v, s.s_sc], inc=[s.s_v])
        self.p.on("gpsimd", lambda e, ap=ap, i=i: e.dma_start(
            out=s.dbg_t[i, 0:ap.shape[0], 0:ap.shape[-1]],
            in_=s.ln_scr[0:ap.shape[0], 0:ap.shape[-1]]),
                  wait=[s.s_v], inc=[(s.s_gd, 16)])
        self.p.on("vector", None, wait=[s.s_gd])

    def finish_early(self, nc):
        self.fence()
        self.p.realize(nc)
        return nc

    def build(self):
        nc = self.nc
        self.declare_io()
        s0 = self.ctx
        self.alloc_persistent(s0)
        ln_scope = ExitStack()
        self.alloc_ln(ln_scope)

        self.emit_startup()
        self.p.realize(nc)
        if self.stop_after == "startup":
            return self.finish_early(nc)

        s1 = ExitStack()
        self.alloc_scan(s1)
        for w in (0, 1):
            s2 = ExitStack()
            self.alloc_window(s2)
            self.emit_window_proj(w)
            self.p.realize(nc)
            if self.stop_after == f"w{w}proj":
                return self.finish_early(nc)
            self.emit_scan(w)
            self.p.realize(nc)
            self.fence(("tensor", "vector", "scalar", "gpsimd"))
            self.p.realize(nc)
            s2.close()
            if self.stop_after and self.stop_after.startswith(f"w{w}scan"):
                return self.finish_early(nc)
        s1.close()

        sDE = ExitStack()
        self.sb(sDE, "h2T", [128, KC * TW], BF16)
        s3 = ExitStack()
        self.alloc_phaseD(s3)
        self.emit_phaseD()
        self.p.realize(nc)
        self.fence(("tensor", "vector", "scalar", "gpsimd"))
        self.p.realize(nc)
        s3.close()
        if self.stop_after == "phaseD":
            return self.finish_early(nc)
        ln_scope.close()

        s4 = ExitStack()
        self.alloc_ffn(s4)
        self.emit_ffnA()
        self.p.realize(nc)
        self.fence(("tensor", "vector", "scalar", "gpsimd"))
        self.p.realize(nc)
        s4.close()
        sDE.close()
        s5 = ExitStack()
        self.alloc_ffnB(s5)
        self.emit_ffnB()
        self.emit_final()
        self.p.realize(nc)
        s5.close()
        return nc

    def declare_io(self):
        d = self.din
        d("xw", [2, TP, D])
        d("c_r", [16, 128])
        d("maskpre", [128, 1])
        d("maskhalo", [128, 1])
        d("wq", [D, 1536]); d("wk", [D, 1536]); d("wv", [D, 3072])
        d("wab", [D, 16])
        d("wg", [D, 3072])
        d("wo", [3072, D])
        d("wmod", [D, 6 * D])
        d("wgt", [D, DHID]); d("wut", [D, DHID]); d("wdt", [DHID, D])
        d("convw", [128, 48 * 4])
        d("alog", [1, 8]); d("dtbv", [1, 8])
        d("normw", [1, DV])
        d("ident", [128, 128]); d("identb", [128, 128], BF16)
        d("cumu", [128, 64])
        d("masks_c", [128, 128]); d("maski_c", [128, 128])
        d("onesf", [1, 128])
        d("constp", [128, 8])
        self.dout("out", [TW, D])
        self.dint("mod_d", [1, 6 * D])
        self.dint("kf_d", [NFK, 128, TW], BF16)
        self.dint("qf_d", [NFQ, 128, TW], BF16)
        self.dint("vtok_d", [NPAIR, 128, 3072], BF16)
        self.dint("gate_d", [NPAIR, 128, 3072], BF16)
        self.dint("o_d", [NPAIR, 128, 3072], BF16)
        self.dint("gu_d", [DHID, TW], BF16)
        self.dint("x1_d", [NPAIR, 128, D])
        if self.debug:
            self.dout("hT_dbg", [2, 128, KC * TP], BF16)
            self.dout("bc_dbg", [2, 128, D])
            self.dout("dbg_t", [32, 128, 512])
            self._dbg_i = 0


    def alloc_persistent(self, sc):
        sb = self.sb
        sb(sc, "id_f", [128, 128], F32)
        sb(sc, "id_b", [128, 128], BF16)
        sb(sc, "cum_u", [128, 64], F32)
        sb(sc, "mskS", [128, 128], F32)
        sb(sc, "mskI", [128, 128], F32)
        sb(sc, "ones128", [128, 128], F32)
        sb(sc, "ones_f", [1, 128], F32)
        sb(sc, "cpool", [128, 8], F32)
        sb(sc, "mskpre_t", [128, 1], F32)
        sb(sc, "mskhalo_t", [128, 1], F32)
        sb(sc, "normw_t", [128, DV], F32)
        sb(sc, "dtb_t", [128, 8], F32)
        sb(sc, "negA_t", [128, 8], F32)
        sb(sc, "convw_t", [128, 192], F32)
        sb(sc, "wab_b", [128, 16 * 16], BF16)
        sb(sc, "silu_ct", [128, 16], BF16)
        sb(sc, "modrow", [1, 512], F32)
        sb(sc, "bcA", [128, D], F32)
        sb(sc, "bcB", [128, D], F32)
        sb(sc, "pan_f", [128, 2 * 2048], F32)
        sb(sc, "pan_b", [128, 2 * 2048], BF16)
        sb(sc, "smallp", [128, 32], F32)
        self.pbank = [sc.enter_context(self.nc.psum_tensor(f"pb{i}", [128, 512], F32))
                      for i in range(6)]
        self.pt = [sc.enter_context(self.nc.psum_tensor(f"pt{i}", [128, 1024], BF16))
                   for i in range(2)]
        for nm in ("s_pe", "s_v", "s_sc", "s_sy", "s_gd", "s_ld", "s_w",
                   "s_v0", "s_v1", "s_c0", "s_c1"):
            self.sem(nm)
        self.bank_last = [(self.s_v, 0) for _ in range(8)]

    def alloc_ln(self, sc):
        sb = self.sb
        sb(sc, "x_tile", [128, D], F32, side="right")
        sb(sc, "ln_scr", [128, D], F32, side="right")
        sb(sc, "h_tile", [128, D], BF16, side="right")
        sb(sc, "ln_sums", [128, 8], F32, side="right")

    def alloc_scan(self, sc):
        sb = self.sb
        sb(sc, "S_sb", [128, 12 * DV], BF16)
        sb(sc, "kfp", [128, 2 * NFK * 128], BF16)
        sb(sc, "qfp", [128, 2 * NFQ * 128], BF16)
        sb(sc, "vpair", [128, 2 * 3072], BF16)
        sb(sc, "gatep", [128, 2 * 3072], BF16)
        sb(sc, "oasm", [128, 3072], BF16)
        sb(sc, "abraw", [128, NPAIR * 16], F32)
        sb(sc, "bwin", [128, NPAIR * 8], F32)
        sb(sc, "nbwin", [128, NPAIR * 8], F32)
        sb(sc, "gwin", [128, NPAIR * 8], F32)
        sb(sc, "gamp", [128, 8], F32)
        sb(sc, "bp", [128, 8], F32)
        sb(sc, "lamg", [128, 16], F32)
        sb(sc, "lamt", [128, 16], F32)
        sb(sc, "eep", [128, 8], F32)
        sb(sc, "nbb", [128, 8], F32)
        sb(sc, "ktok", [128, 2 * DK], BF16)
        sb(sc, "ketok6", [128, 6 * DK], BF16)
        sb(sc, "nsq", [128, 48], F32)
        sb(sc, "sq_scr", [128, 2 * 256], F32)
        sb(sc, "E_sb", [128, 2 * 128], F32)
        sb(sc, "AX1", [128, 2 * 128], BF16)
        sb(sc, "AXT", [128, 2 * 128], BF16)
        sb(sc, "APT6", [128, 6 * 128], BF16)
        sb(sc, "DqK", [128, 2 * 128], BF16)
        sb(sc, "MuT6", [128, 6 * 128], BF16)
        sb(sc, "lamI", [128, 6 * 256], BF16)
        sb(sc, "bV", [128, 2 * DV], F32)
        sb(sc, "w_sb", [128, 2 * DV], BF16)
        sb(sc, "u_bf", [128, 2 * DV], BF16)
        sb(sc, "t2", [128, 2 * DV], F32)

    def alloc_window(self, sc):
        sb = self.sb
        sb(sc, "hT", [128, KC * TP], BF16)
        sb(sc, "praw", [128, 2 * TP], BF16)
        sb(sc, "cacc", [128, 2 * TW], F32)
        sb(sc, "vasm", [128, NPAIR * 128], BF16)

    def alloc_phaseD(self, sc):
        sb = self.sb
        sb(sc, "oT", [128, 24 * TW], BF16)
        sb(sc, "otile", [128, 3072], BF16)
        sb(sc, "wopan", [128, 24 * 512], BF16)
        sb(sc, "res_t", [128, D], F32)

    def alloc_ffn(self, sc):
        sb = self.sb
        sb(sc, "fpan_f", [128, 2 * 2048], F32)
        sb(sc, "fpan_b", [128, 2 * 2048], BF16)
        sb(sc, "gu_sb", [128, 2 * TW], BF16)

    def alloc_ffnB(self, sc):
        sb = self.sb
        sb(sc, "wdpan", [128, 64 * 512], BF16)
        sb(sc, "gucol", [128, 2 * 64 * 128], BF16)
        sb(sc, "res2_t", [128, 2048], F32)

    def emit_startup(self):
        p, s = self.p, self

        def ld(dst, src, wait=()):
            p.on("sync", lambda e, d=dst, sr=src: e.dma_start(out=d, in_=sr),
                 wait=wait, inc=[(s.s_ld, 16)])

        ld(s.id_f[:], s.ident[:])
        ld(s.id_b[:], s.identb[:])
        ld(s.cum_u[:], s.cumu[:])
        ld(s.mskS[:], s.masks_c[:])
        ld(s.mskI[:], s.maski_c[:])
        ld(s.ones_f[:], s.onesf[:])
        ld(s.cpool[:], s.constp[:])
        ld(s.mskpre_t[:], s.maskpre[:])
        ld(s.mskhalo_t[:], s.maskhalo[:])
        ld(s.convw_t[:], s.convw[:])
        ld(s.ln_scr[0:16, 0:128], s.c_r[:])
        p.on("vector", lambda e: e.memset(s.ones128[:], 1.0), inc=[s.s_v])
        ld(s.ln_scr[0:1, 256:256 + DV], s.normw[:])
        ld(s.ln_scr[0:1, 768:776], s.alog[:])
        ld(s.ln_scr[0:1, 776:784], s.dtbv[:])
        ld(s.pan_f[:, 0:256], s.wab[:].rearrange("(kc p) m -> p kc m", p=128))

        p.on("scalar", lambda e: e.activation(out=s.ln_scr[0:16, 128:256],
                                              in_=s.ln_scr[0:16, 0:128], func=AF.Silu),
             wait=[s.s_ld], inc=[s.s_sc])
        p.on("tensor", lambda e: e.transpose(out=s.pbank[0][:, 0:16],
                                             in_=s.ln_scr[0:16, 128:256],
                                             identity=s.id_f[0:16, 0:16]),
             wait=[s.s_sc], inc=[s.s_pe])
        s.evac("vector", lambda e: e.tensor_copy(out=s.silu_ct[:], in_=s.pbank[0][:, 0:16]),
               bank=0)
        p.on("vector", lambda e: e.tensor_copy(out=s.wab_b[:], in_=s.pan_f[:, 0:256]),
             wait=[s.s_ld], inc=[s.s_v])
        s.pe_mm(s.pbank[1][:, 0:512], s.ones_f[:], s.ln_scr[0:1, 256:256 + DV],
                bank=1, start=True, stop=True, wait=[s.s_ld])
        s.evac("vector", lambda e: e.tensor_copy(out=s.normw_t[:], in_=s.pbank[1][:, 0:512]),
               bank=1)
        s.pe_mm(s.pbank[0][:, 0:8], s.ones_f[:], s.ln_scr[0:1, 768:776],
                bank=0, start=True, stop=True)
        s.pe_mm(s.pbank[0][:, 8:16], s.ones_f[:], s.ln_scr[0:1, 776:784],
                bank=0, start=False, stop=True)
        s.evac("scalar", lambda e: e.activation(out=s.negA_t[:], in_=s.pbank[0][:, 0:8],
                                                func=AF.Exp), bank=0)
        p.on("vector", lambda e: e.tensor_scalar_mul(out=s.negA_t[:], in0=s.negA_t[:],
                                                     scalar1=-1.0),
             wait=[s.s_sc], inc=[s.s_v])
        s.evac("vector", lambda e: e.tensor_copy(out=s.dtb_t[:], in_=s.pbank[0][:, 8:16]),
               bank=0, wait=[s.s_sc])

        def stage_mod(j):
            half = j % 2
            wwar = [s.s_pe] if j >= 2 else [s.s_v]
            p.on("sync", lambda e, j=j, half=half: e.dma_start(
                out=s.pan_f[:, half * 2048:(half + 1) * 2048],
                in_=s.wmod[:, j * 128:(j + 1) * 128].rearrange("(kc p) m -> p kc m", p=128)),
                 wait=wwar, inc=[(s.s_ld, 16)])
            p.on("vector", lambda e, half=half: e.tensor_copy(
                out=s.pan_b[:, half * 2048:(half + 1) * 2048],
                in_=s.pan_f[:, half * 2048:(half + 1) * 2048]),
                 wait=[s.s_ld, s.s_pe], inc=[s.s_w])
        stage_mod(0)
        stage_mod(1)
        for j in range(96):
            half = j % 2
            bank = 2 + half
            for kc in range(KC):
                s.pe_mm(s.pbank[bank][0:1, 0:128], s.silu_ct[:, kc:kc + 1],
                        s.pan_b[:, half * 2048 + kc * 128: half * 2048 + (kc + 1) * 128],
                        bank=bank, start=(kc == 0), stop=(kc == 15),
                        wait=[s.s_w] + ([s.s_v] if j == 0 else []),
                        inc_pe=(kc == 15))
            if j + 2 < 96:
                stage_mod(j + 2)
            s.evac("vector", lambda e, bank=bank, j=j: e.tensor_copy(
                out=s.modrow[0:1, (j % 4) * 128:(j % 4) * 128 + 128],
                in_=s.pbank[bank][0:1, 0:128]),
                   bank=bank, wait=[s.s_gd])
            if j % 4 == 3:
                p.on("gpsimd", lambda e, j=j: e.dma_start(
                    out=s.mod_d[:, (j - 3) * 128:(j + 1) * 128], in_=s.modrow[0:1, 0:512]),
                     wait=[s.s_v], inc=[(s.s_gd, 16)])

    def emit_mod_bcast(self, which, scr=None):
        p, s = self.p, self
        if scr is None:
            scr = self.ln_scr
        def one_vec(src_off, dst, fused):
            for t in range(4):
                p.on("sync", lambda e, t=t: e.dma_start(
                    out=s.modrow[0:1, 0:512],
                    in_=s.mod_d[:, src_off + t * 512:src_off + (t + 1) * 512]),
                     wait=[s.s_sc, s.s_v, s.s_gd], inc=[(s.s_ld, 16)])
                s.pe_mm(s.pbank[4][:, 0:512], s.ones_f[:], s.modrow[0:1, 0:512],
                        bank=4, start=True, stop=True, wait=[s.s_ld])
                fused(t, dst)
        if which in ("a", "b"):
            so = 0 if which == "a" else 3 * D
            co = D if which == "a" else 4 * D
            one_vec(so, s.bcB, lambda t, dst: s.evac(
                "vector", lambda e, t=t: e.tensor_copy(
                    out=dst[:, t * 512:(t + 1) * 512], in_=s.pbank[4][:, 0:512]), bank=4))
            one_vec(co, s.bcA, lambda t, dst: s.evac(
                "vector", lambda e, t=t: e.tensor_scalar_add(
                    out=dst[:, t * 512:(t + 1) * 512], in0=s.pbank[4][:, 0:512],
                    scalar1=1.0), bank=4))
        else:
            go = 2 * D if which == "ga" else 5 * D
            def clamp(t, dst):
                sl = slice(t * 512, (t + 1) * 512)
                s.evac("scalar", lambda e: e.activation(out=scr[:, 0:512],
                                                        in_=s.pbank[4][:, 0:512],
                                                        func=AF.Abs), bank=4)
                s.evac("scalar", lambda e: e.sign(out=scr[:, 512:1024],
                                                  in_=s.pbank[4][:, 0:512]), bank=4)
                p.on("vector", lambda e: e.tensor_scalar_max(
                    out=scr[:, 0:512], in0=scr[:, 0:512], scalar1=0.25),
                     wait=[s.s_sc], inc=[s.s_v])
                p.on("vector", lambda e, sl=sl: e.tensor_mul(
                    out=dst[:, sl], in0=scr[:, 0:512], in1=scr[:, 512:1024]),
                     wait=[s.s_sc], inc=[s.s_v])
            one_vec(go, s.bcA, clamp)

    def emit_ln_tile(self, xsrc, dstT, dst_stride, tok_off, last_readers):
        p, s = self.p, self
        p.on("sync", lambda e: e.dma_start(out=s.x_tile[:], in_=xsrc),
             wait=last_readers, inc=[(s.s_ld, 16)])
        p.on("vector", lambda e: e.reduce_sum(out=s.ln_sums[:, 0:1], in_=s.x_tile[:],
                                              axis=AX.X),
             wait=[s.s_ld], inc=[s.s_v])
        p.on("scalar", lambda e: e.activation(out=s.ln_scr[:], in_=s.x_tile[:],
                                              func=AF.Square,
                                              accum_out=s.ln_sums[:, 1:2]),
             wait=[s.s_ld], inc=[s.s_sc])
        p.on("scalar", lambda e: e.activation(out=s.ln_sums[:, 4:5],
                                              in_=s.ln_sums[:, 1:2], func=AF.Copy,
                                              scale=1.0 / D), inc=[s.s_sc])
        p.on("vector", lambda e: e.tensor_scalar_mul(out=s.ln_sums[:, 2:3],
                                                     in0=s.ln_sums[:, 0:1],
                                                     scalar1=1.0 / D),
             wait=[s.s_v, s.s_sc], inc=[s.s_v])
        p.on("vector", lambda e: e.tensor_mul(out=s.ln_sums[:, 3:4],
                                              in0=s.ln_sums[:, 2:3],
                                              in1=s.ln_sums[:, 2:3]),
             wait=[s.s_v], inc=[s.s_v])
        p.on("vector", lambda e: e.tensor_sub(out=s.ln_sums[:, 5:6],
                                              in0=s.ln_sums[:, 4:5],
                                              in1=s.ln_sums[:, 3:4]),
             wait=[s.s_v, s.s_sc], inc=[s.s_v])
        p.on("scalar", lambda e: e.activation(out=s.ln_sums[:, 6:7], in_=s.ln_sums[:, 5:6],
                                              func=AF.Ln, bias=s.cpool[:, 0:1]),
             wait=[s.s_v], inc=[s.s_sc])
        p.on("scalar", lambda e: e.activation(out=s.ln_sums[:, 7:8], in_=s.ln_sums[:, 6:7],
                                              func=AF.Exp, scale=-0.5), inc=[s.s_sc])
        p.on("vector", lambda e: e.scalar_tensor_tensor(
            out=s.ln_scr[:], in0=s.x_tile[:], scalar=s.ln_sums[:, 2:3], in1=s.bcA[:],
            op0=ALU.subtract, op1=ALU.mult),
             wait=[s.s_sc, s.s_v], inc=[s.s_v])
        p.on("vector", lambda e: e.scalar_tensor_tensor(
            out=s.h_tile[:], in0=s.ln_scr[:], scalar=s.ln_sums[:, 7:8], in1=s.bcB[:],
            op0=ALU.mult, op1=ALU.add),
             wait=[s.s_sc], inc=[s.s_v])
        for g in (0, 1):
            bank = 6 + g
            ptb = self.pt[g]
            for f8 in range(8):
                fc = g * 8 + f8
                s.pe_tr(ptb[:, f8 * 128:(f8 + 1) * 128],
                        s.h_tile[:, fc * 128:(fc + 1) * 128], s.id_b[:],
                        bank=bank, wait=[s.s_v])
            eng = "vector" if g == 0 else "scalar"
            for f8 in range(8):
                fc = g * 8 + f8
                dsl = dstT[:, fc * dst_stride + tok_off: fc * dst_stride + tok_off + 128]
                psl = ptb[:, f8 * 128:(f8 + 1) * 128]
                if eng == "vector":
                    s.evac(eng, lambda e, d=dsl, ps=psl: e.tensor_copy(out=d, in_=ps),
                           bank=bank)
                else:
                    s.evac(eng, lambda e, d=dsl, ps=psl: e.copy(out=d, in_=ps), bank=bank)

    def emit_window_proj(self, w):
        p, s = self.p, self
        self.emit_mod_bcast("a")
        for t in range(NT):
            self.emit_ln_tile(s.xw[w, t * 128:(t + 1) * 128, :], s.hT, TP, t * 128,
                              last_readers=[s.s_v, s.s_sc])
        if self.debug:
            p.on("gpsimd", lambda e: e.dma_start(out=s.hT_dbg[w], in_=s.hT[:]),
                 wait=[s.s_v, s.s_sc], inc=[(s.s_gd, 16)])
            if w == 1:
                p.on("gpsimd", lambda e: e.dma_start(out=s.bc_dbg[0], in_=s.bcA[:]),
                     wait=[s.s_v, s.s_sc], inc=[(s.s_gd, 16)])
                p.on("gpsimd", lambda e: e.dma_start(out=s.bc_dbg[1], in_=s.bcB[:]),
                     wait=[s.s_v, s.s_sc], inc=[(s.s_gd, 16)])
        fcs = []
        if w == 1:
            fcs += [("q", i) for i in range(NFQ)]
        fcs += [("k", i) for i in range(NFK)]
        fcs += [("v", i) for i in range(NFV)]
        if w == 1:
            fcs += [("g", i) for i in range(NFG)]
        wsrc = {"q": s.wq, "k": s.wk, "v": s.wv, "g": s.wg}

        def stage_proj(idx):
            kind, fc = fcs[idx]
            half = idx % 2
            po = half * 2048
            p.on("sync", lambda e, kind=kind, fc=fc, po=po: e.dma_start(
                out=s.pan_f[:, po:po + 2048],
                in_=wsrc[kind][:, fc * 128:(fc + 1) * 128]
                .rearrange("(kc p) m -> p kc m", p=128)),
                 wait=[s.s_pe], inc=[(s.s_ld, 16)])
            p.on("vector", lambda e, po=po: e.tensor_copy(
                out=s.pan_b[:, po:po + 2048], in_=s.pan_f[:, po:po + 2048]),
                 wait=[s.s_ld, s.s_pe], inc=[s.s_w])
        if fcs:
            stage_proj(0)
            if len(fcs) > 1:
                stage_proj(1)
        for idx, (kind, fc) in enumerate(fcs):
            half = idx % 2
            po, pb = half * 2048, half * 2048
            span = TP if kind in ("q", "k", "v") else TW
            toff = 0 if kind in ("q", "k", "v") else 128
            nmt = 3 if span == TP else 2
            mw = span // nmt
            ph = half
            for mt in range(nmt):
                bank = (idx * 3 + mt) % 6
                for kc in range(KC):
                    s.pe_mm(self.pbank[bank][:, 0:mw],
                            s.pan_b[:, pb + kc * 128:pb + (kc + 1) * 128],
                            s.hT[:, kc * TP + toff + mt * mw: kc * TP + toff + (mt + 1) * mw],
                            bank=bank, start=(kc == 0), stop=(kc == 15),
                            wait=([s.s_w] + ([s.s_v, s.s_sc] if idx == 0 else []))
                            if kc == 0 else [], inc_pe=(kc == 15))
                if kind in ("q", "k", "v"):
                    dsl = s.praw[:, ph * TP + mt * mw: ph * TP + (mt + 1) * mw]
                    s.evac("scalar", lambda e, d=dsl, bank=bank, mw=mw: e.copy(
                        out=d, in_=self.pbank[bank][:, 0:mw]), bank=bank,
                           wait=[s.s_v, s.s_gd])
                else:
                    dsl = s.cacc[:, ph * TW + mt * mw: ph * TW + (mt + 1) * mw]
                    s.evac("scalar", lambda e, d=dsl, bank=bank, mw=mw: e.activation(
                        out=d, in_=self.pbank[bank][:, 0:mw], func=AF.Silu), bank=bank,
                           wait=[s.s_v])
            if idx + 2 < len(fcs):
                stage_proj(idx + 2)
            if kind in ("q", "k", "v"):
                if w == 0:
                    p.on("scalar", lambda e, ph=ph: e.memzero(
                        s.praw[:, ph * TP + 124: ph * TP + 128]), inc=[s.s_sc])
                else:
                    p.on("scalar", lambda e, ph=ph: e.activation(
                        out=s.praw[:, ph * TP + 124: ph * TP + 128],
                        in_=s.praw[:, ph * TP + 124: ph * TP + 128],
                        func=AF.Copy, scale=s.mskhalo_t[:, 0:1]), inc=[s.s_sc])
                cw = s.convw_t
                cwbase = {"q": 0, "k": NFQ, "v": NFQ + NFK}[kind] + fc
                ca = s.cacc[:, ph * TW:(ph + 1) * TW]
                p.on("vector", lambda e, ph=ph, cwbase=cwbase, ca=ca: e.tensor_scalar_mul(
                    out=ca, in0=s.praw[:, ph * TP + 125: ph * TP + 125 + TW],
                    scalar1=cw[:, cwbase * 4:cwbase * 4 + 1]),
                     wait=[s.s_sc], inc=[s.s_v])
                for i in (1, 2, 3):
                    p.on("vector", lambda e, ph=ph, cwbase=cwbase, i=i, ca=ca:
                         e.scalar_tensor_tensor(
                             out=ca, in0=s.praw[:, ph * TP + 125 + i: ph * TP + 125 + i + TW],
                             scalar=cw[:, cwbase * 4 + i:cwbase * 4 + i + 1], in1=ca,
                             op0=ALU.mult, op1=ALU.add), inc=[s.s_v])
                if kind in ("q", "k"):
                    p.on("scalar", lambda e, ph=ph, ca=ca: e.activation(
                        out=s.praw[:, ph * TP:ph * TP + TW], in_=ca, func=AF.Silu),
                         wait=[s.s_v], inc=[s.s_sc])
                    dd = s.kf_d if kind == "k" else s.qf_d
                    p.on("gpsimd", lambda e, dd=dd, fc=fc, ph=ph: e.dma_start(
                        out=dd[fc], in_=s.praw[:, ph * TP:ph * TP + TW]),
                         wait=[s.s_sc], inc=[(s.s_gd, 16)])
                else:
                    p.on("scalar", lambda e, ph=ph, ca=ca: e.activation(
                        out=s.praw[:, ph * TP:ph * TP + TW], in_=ca, func=AF.Silu),
                         wait=[s.s_v], inc=[s.s_sc])
                    self._transpose_to_pairs(s.praw[:, ph * TP:ph * TP + TW],
                                             s.vtok_d, fc, wait=[s.s_sc])
            else:
                p.on("vector", lambda e, ph=ph: e.tensor_copy(
                    out=s.praw[:, ph * TP:ph * TP + TW], in_=s.cacc[:, ph * TW:(ph + 1) * TW]),
                     wait=[s.s_sc], inc=[s.s_v])
                self._transpose_to_pairs(s.praw[:, ph * TP:ph * TP + TW],
                                         s.gate_d, fc, wait=[s.s_v])
        for tt in range(NPAIR):
            bank = 4
            for kc in range(KC):
                s.pe_mm(self.pbank[bank][:, 0:16],
                        s.hT[:, kc * TP + 128 + tt * 128: kc * TP + 256 + tt * 128],
                        s.wab_b[:, kc * 16:(kc + 1) * 16],
                        bank=bank, start=(kc == 0), stop=(kc == 15),
                        wait=[s.s_v, s.s_sc] if kc == 0 else [], inc_pe=(kc == 15))
            s.evac("vector", lambda e, tt=tt: e.tensor_copy(
                out=s.abraw[:, tt * 16:(tt + 1) * 16], in_=self.pbank[bank][:, 0:16]),
                   bank=bank)
            ab = s.abraw[:, tt * 16:tt * 16 + 6]
            bb = s.abraw[:, tt * 16 + 8:tt * 16 + 14]
            p.on("scalar", lambda e, bb=bb, tt=tt: e.activation(
                out=s.bwin[:, tt * 8:tt * 8 + 6], in_=bb, func=AF.Sigmoid),
                 wait=[s.s_v], inc=[s.s_sc])
            p.on("vector", lambda e, tt=tt: e.tensor_scalar_mul(
                out=s.nbwin[:, tt * 8:tt * 8 + 6], in0=s.bwin[:, tt * 8:tt * 8 + 6],
                scalar1=-1.0), wait=[s.s_sc], inc=[s.s_v])
            p.on("vector", lambda e, ab=ab: e.tensor_add(
                out=s.smallp[:, 0:6], in0=ab, in1=s.dtb_t[:, 0:6]),
                 wait=[s.s_v], inc=[s.s_v])
            p.on("vector", lambda e: e.tensor_relu(out=s.smallp[:, 8:14],
                                                   in_=s.smallp[:, 0:6]),
                 wait=[s.s_v], inc=[s.s_v])
            p.on("scalar", lambda e: e.activation(out=s.smallp[:, 16:22],
                                                  in_=s.smallp[:, 0:6], func=AF.Abs),
                 wait=[s.s_v], inc=[s.s_sc])
            p.on("scalar", lambda e: e.activation(out=s.smallp[:, 16:22],
                                                  in_=s.smallp[:, 16:22], func=AF.Exp,
                                                  scale=-1.0), inc=[s.s_sc])
            p.on("scalar", lambda e: e.activation(out=s.smallp[:, 16:22],
                                                  in_=s.smallp[:, 16:22], func=AF.Ln,
                                                  bias=s.cpool[:, 2:3]), inc=[s.s_sc])
            p.on("vector", lambda e: e.tensor_add(out=s.smallp[:, 8:14],
                                                  in0=s.smallp[:, 8:14],
                                                  in1=s.smallp[:, 16:22]),
                 wait=[s.s_sc, s.s_v], inc=[s.s_v])
            p.on("vector", lambda e, tt=tt: e.tensor_mul(
                out=s.gwin[:, tt * 8:tt * 8 + 6], in0=s.smallp[:, 8:14],
                in1=s.negA_t[:, 0:6]), wait=[s.s_v], inc=[s.s_v])

    def _transpose_to_pairs(self, src_bf, dram_dst, fc, wait):
        p, s = self.p, self
        for g in (0, 1):
            bank = 6 + g
            ptb = self.pt[g]
            for f4 in range(4):
                pr = g * 4 + f4
                s.pe_tr(ptb[:, f4 * 128:(f4 + 1) * 128],
                        src_bf[:, pr * 128:(pr + 1) * 128], s.id_b[:],
                        bank=bank, wait=wait if (pr == 0) else [])
            eng = "vector" if g == 0 else "scalar"
            for f4 in range(4):
                pr = g * 4 + f4
                dsl = s.vasm[:, pr * 128:(pr + 1) * 128]
                psl = ptb[:, f4 * 128:(f4 + 1) * 128]
                war = [s.s_gd] if f4 == 0 else []
                if eng == "vector":
                    s.evac(eng, lambda e, d=dsl, ps=psl: e.tensor_copy(out=d, in_=ps),
                           bank=bank, wait=war)
                else:
                    s.evac(eng, lambda e, d=dsl, ps=psl: e.copy(out=d, in_=ps),
                           bank=bank, wait=war)
        p.on("gpsimd", lambda e: e.dma_start(
            out=dram_dst[:, :, fc * 128:(fc + 1) * 128].rearrange("pr p m -> p pr m"),
            in_=s.vasm[:].rearrange("p (pr m) -> p pr m", pr=8)),
             wait=[s.s_v, s.s_sc], inc=[(s.s_gd, 16)])

    def emit_scan(self, w):
        p, s = self.p, self
        full = (w == 1)
        if w == 0:
            p.on("vector", lambda e: e.memset(s.S_sb[:], 0.0), wait=[s.s_pe], inc=[s.s_v])
        else:
            p.on("vector", lambda e: e.tensor_scalar_mul(
                out=s.S_sb[:], in0=s.S_sb[:], scalar1=s.mskpre_t[:, 0:1]),
                 wait=[s.s_pe], inc=[s.s_v])
        npair, nhead = NPAIR, H
        if self.stop_after and self.stop_after.startswith(f"w{w}scan:"):
            parts = self.stop_after.split(":")
            npair = int(parts[1])
            nhead = int(parts[2]) if len(parts) > 2 else H
        for pr in range(npair):
            self.emit_scan_pair(w, pr, full,
                                nhead if pr == npair - 1 else H)

    def emit_scan_pair(self, w, pr, full, nhead=H):
        p, s = self.p, self
        sub = 99
        if nhead < 0:
            sub, nhead = -nhead, 0
        half = pr % 2
        kslot = s.kfp[:, half * NFK * 128:(half + 1) * NFK * 128]
        qslot = s.qfp[:, half * NFQ * 128:(half + 1) * NFQ * 128]
        vslot = s.vpair[:, half * 3072:(half + 1) * 3072]
        gslot = s.gatep[:, half * 3072:(half + 1) * 3072]
        war = [s.s_pe, s.s_v0, s.s_v1] if pr >= 2 else [s.s_gd]
        p.on("sync", lambda e: e.dma_start(
            out=kslot, in_=s.kf_d[:, :, pr * 128:(pr + 1) * 128]
            .rearrange("f p m -> p f m")), wait=war, inc=[(s.s_ld, 16)])
        p.on("sync", lambda e: e.dma_start(
            out=vslot, in_=s.vtok_d[pr].rearrange("p m -> p m")), wait=war,
             inc=[(s.s_ld, 16)])
        if full:
            p.on("sync", lambda e: e.dma_start(
                out=qslot, in_=s.qf_d[:, :, pr * 128:(pr + 1) * 128]
                .rearrange("f p m -> p f m")), wait=war, inc=[(s.s_ld, 16)])
            p.on("sync", lambda e: e.dma_start(
                out=gslot, in_=s.gate_d[pr].rearrange("p m -> p m")), wait=war,
                 inc=[(s.s_ld, 16)])
        ld_target = s.s_ld.n
        if sub < 2:
            return

        pc = pr * 8
        for c in (0, 1):
            s.pe_mm(self.pbank[0][c * 64:(c + 1) * 64, 0:6],
                    s.cum_u[c * 64:(c + 1) * 64, 0:64],
                    s.gwin[c * 64:(c + 1) * 64, pc:pc + 6],
                    bank=0, start=True, stop=True,
                    wait=[s.s_v] if c == 0 else [], inc_pe=(c == 1))
        s.evac("vector", lambda e: e.tensor_copy(out=s.gamp[:, 0:6],
                                                 in_=self.pbank[0][:, 0:6]), bank=0)
        s.evac("scalar", lambda e: e.activation(out=s.bp[:, 0:6],
                                                in_=self.pbank[0][:, 0:6], func=AF.Exp),
               bank=0, wait=[s.s_v])
        if sub < 3:
            return
        p.on("vector", lambda e: e.tensor_mul(out=s.nbb[:, 0:6],
                                              in0=s.nbwin[:, pc:pc + 6],
                                              in1=s.bp[:, 0:6]),
             wait=[s.s_v, s.s_sc], inc=[s.s_v])

        for h in range(nhead):
            self.emit_scan_head_A(w, pr, h, half, full, kslot, qslot, ld_target)
        for h in range(nhead):
            self.emit_scan_head_B(w, pr, h, half, full, kslot, qslot, vslot, gslot,
                                  ld_target)
        if full:
            p.on("gpsimd", lambda e: e.dma_start(out=s.o_d[pr], in_=s.oasm[:]),
                 wait=[s.s_v0, s.s_v1], inc=[(s.s_gd, 16)])

    def emit_scan_head_A(self, w, pr, h, half, full, kslot, qslot, ld_tgt):
        p, s = self.p, self
        pc = pr * 8
        par = h % 2
        vp = s.s_v0 if par == 0 else s.s_v1
        scp = s.s_c0 if par == 0 else s.s_c1
        g1 = [s.s_v, s.s_sc] if h < 2 else []
        ptb = self.pt[par]
        bkk = 0 if par == 0 else 2
        binv = 1 if par == 0 else 3
        ksl = [kslot[:, (2 * h + d) * 128:(2 * h + d + 1) * 128] for d in (0, 1)]
        ko = par * DK
        nq = s.nsq[:, h * 8:h * 8 + 8]
        Es = s.E_sb[:, par * 128:(par + 1) * 128]
        ax = s.AX1[:, par * 128:(par + 1) * 128]
        axt = s.AXT[:, par * 128:(par + 1) * 128]
        apt = s.APT6[:, h * 128:(h + 1) * 128]
        dqk = s.DqK[:, par * 128:(par + 1) * 128]
        mut = s.MuT6[:, h * 128:(h + 1) * 128]
        ke = s.ketok6[:, h * DK:(h + 1) * DK]
        sqs = s.sq_scr[:, par * 256:(par + 1) * 256]
        for d in (0, 1):
            s.pe_tr(ptb[:, d * 128:(d + 1) * 128], ksl[d], s.id_b[:], bank=6 + par,
                    wait=[s.s_ld] if h < 2 else [])
        for d in (0, 1):
            s.evac("vector", lambda e, d=d: e.tensor_copy(
                out=s.ktok[:, ko + d * 128:ko + (d + 1) * 128],
                in_=ptb[:, d * 128:(d + 1) * 128]), bank=6 + par, sem=vp)
        p.on("scalar", lambda e: e.activation(out=sqs, in_=s.ktok[:, ko:ko + 256],
                                              func=AF.Square, accum_out=nq[:, 0:1]),
             wait=[vp], inc=[scp])
        p.on("scalar", lambda e: e.activation(out=nq[:, 1:2], in_=nq[:, 0:1],
                                              func=AF.Ln), inc=[scp])
        p.on("scalar", lambda e: e.activation(out=nq[:, 2:3], in_=nq[:, 1:2],
                                              func=AF.Exp, scale=-0.5), inc=[scp])
        p.on("vector", lambda e: e.tensor_scalar_mul(out=s.ktok[:, ko:ko + 256],
                                                     in0=s.ktok[:, ko:ko + 256],
                                                     scalar1=nq[:, 2:3]),
             wait=[scp], inc=[vp])
        for d in (0, 1):
            s.pe_tr(ptb[:, (2 + d) * 128:(3 + d) * 128],
                    s.ktok[:, ko + d * 128:ko + (d + 1) * 128], s.id_b[:],
                    bank=6 + par, wait=[vp])
        for d in (0, 1):
            s.evac("vector", lambda e, d=d: e.tensor_copy(
                out=ksl[d], in_=ptb[:, (2 + d) * 128:(3 + d) * 128]), bank=6 + par, sem=vp)
        if full:
            qsl = [qslot[:, (2 * h + d) * 128:(2 * h + d + 1) * 128] for d in (0, 1)]
            for d in (0, 1):
                s.pe_tr(ptb[:, (4 + d) * 128:(5 + d) * 128], qsl[d], s.id_b[:],
                        bank=6 + par, wait=[s.s_ld] if h < 2 else [])
            for d in (0, 1):
                s.evac("vector", lambda e, d=d: e.tensor_copy(
                    out=sqs[:, d * 128:(d + 1) * 128],
                    in_=ptb[:, (4 + d) * 128:(5 + d) * 128]), bank=6 + par, sem=vp)
            p.on("scalar", lambda e: e.activation(out=sqs, in_=sqs, func=AF.Square,
                                                  accum_out=nq[:, 4:5]),
                 wait=[vp], inc=[scp])
            p.on("scalar", lambda e: e.activation(out=nq[:, 5:6], in_=nq[:, 4:5],
                                                  func=AF.Ln), inc=[scp])
            p.on("scalar", lambda e: e.activation(out=nq[:, 6:7], in_=nq[:, 5:6],
                                                  func=AF.Exp, scale=-0.5,
                                                  bias=s.cpool[:, 1:2]), inc=[scp])
            p.on("vector", lambda e: e.tensor_mul(out=nq[:, 7:8],
                                                  in0=s.bp[:, h:h + 1],
                                                  in1=nq[:, 6:7]),
                 wait=[scp] + g1, inc=[vp])
        for d in (0, 1):
            s.pe_mm(self.pbank[bkk][:, 0:128], ksl[d], ksl[d],
                    bank=bkk, start=(d == 0), stop=(d == 1),
                    wait=[vp] if d == 0 else [], inc_pe=(not full) and d == 1)
        if full:
            qsl = [qslot[:, (2 * h + d) * 128:(2 * h + d + 1) * 128] for d in (0, 1)]
            for d in (0, 1):
                s.pe_mm(self.pbank[bkk][:, 128:256], qsl[d], ksl[d],
                        bank=bkk, start=(d == 0), stop=(d == 1), inc_pe=(d == 1))
        p.on("vector", lambda e: e.tensor_scalar_mul(
            out=s.sq_scr[:, par * 256:par * 256 + 128], in0=s.ones128[:, 0:128],
            scalar1=s.gamp[:, h:h + 1]),
             wait=[vp, scp, s.s_v], inc=[vp])
        p.on("tensor", lambda e: e.transpose(
            out=self.pbank[binv][:, 0:128], in_=s.sq_scr[:, par * 256:par * 256 + 128],
            identity=s.id_f[:]),
             wait=[vp], wait_at=[self.bank_last[binv]], inc=[self.s_pe])
        s.evac("vector", lambda e: e.tensor_copy(
            out=s.lamg[:, 2 * h:2 * h + 1], in_=self.pbank[binv][:, 63:64]), bank=binv, sem=vp)
        s.evac("vector", lambda e: e.tensor_copy(
            out=s.lamg[:, 2 * h + 1:2 * h + 2], in_=self.pbank[binv][:, 127:128]),
               bank=binv, sem=vp)
        p.on("scalar", lambda e: e.activation(out=s.lamt[:, 2 * h:2 * h + 2],
                                              in_=s.lamg[:, 2 * h:2 * h + 2],
                                              func=AF.Exp), wait=[vp], inc=[scp])
        p.on("vector", lambda e: e.tensor_sub(
            out=s.eep[0:64, h:h + 1], in0=s.lamg[0:64, 2 * h:2 * h + 1],
            in1=s.gamp[0:64, h:h + 1]), wait=[vp], inc=[vp])
        p.on("vector", lambda e: e.tensor_sub(
            out=s.eep[64:128, h:h + 1], in0=s.lamg[64:128, 2 * h + 1:2 * h + 2],
            in1=s.gamp[64:128, h:h + 1]), inc=[vp])
        p.on("scalar", lambda e: e.activation(out=s.eep[:, h:h + 1],
                                              in_=s.eep[:, h:h + 1], func=AF.Exp),
             wait=[vp], inc=[scp])
        p.on("vector", lambda e: e.tensor_scalar_mul(out=ke, in0=s.ktok[:, ko:ko + 256],
                                                     scalar1=s.eep[:, h:h + 1]),
             wait=[scp], inc=[vp])
        for c in (0, 1):
            p.on("vector", lambda e, c=c: e.tensor_scalar_mul(
                out=s.lamI[:, h * 256 + c * 128:h * 256 + (c + 1) * 128],
                in0=s.id_f[:],
                scalar1=s.lamt[:, 2 * h + c:2 * h + c + 1]),
                 wait=[scp], inc=[vp])
        s.evac("vector", lambda e: e.tensor_scalar(
            out=Es, in0=self.pbank[binv][:, 0:128], scalar1=s.gamp[:, h:h + 1],
            scalar2=0.0, op0=ALU.subtract, op1=ALU.max), bank=binv, wait=[vp], sem=vp)
        p.on("scalar", lambda e: e.activation(out=Es, in_=Es, func=AF.Exp,
                                              scale=-1.0), wait=[vp], inc=[scp])
        s.evac("vector", lambda e: e.tensor_mul(out=sqs[:, 0:128], in0=Es,
                                                in1=self.pbank[bkk][:, 0:128]),
               bank=bkk, wait=[scp], sem=vp)
        p.on("vector", lambda e: e.tensor_mul(out=sqs[:, 0:128],
                                              in0=sqs[:, 0:128], in1=s.mskS[:]),
             inc=[vp])
        p.on("vector", lambda e: e.tensor_scalar_mul(out=ax, in0=sqs[:, 0:128],
                                                     scalar1=s.nbwin[:, pc + h:pc + h + 1]),
             inc=[vp])
        if full:
            s.evac("vector", lambda e: e.scalar_tensor_tensor(
                out=sqs[:, 128:256], in0=self.pbank[bkk][:, 128:256],
                scalar=nq[:, 6:7], in1=Es, op0=ALU.mult, op1=ALU.mult),
                   bank=bkk, wait=[scp], sem=vp)
            p.on("vector", lambda e: e.tensor_mul(out=dqk, in0=sqs[:, 128:256],
                                                  in1=s.mskI[:]), inc=[vp])
            s.pe_tr(ptb[:, 6 * 128:7 * 128], dqk, s.id_b[:], bank=6 + par,
                    wait=[vp])
            s.evac("vector", lambda e: e.tensor_copy(out=mut,
                                                     in_=ptb[:, 6 * 128:7 * 128]),
                   bank=6 + par)
        s.pe_tr(ptb[:, 7 * 128:8 * 128], ax, s.id_b[:], bank=6 + par, wait=[vp])
        s.evac("vector", lambda e: e.tensor_copy(out=axt, in_=ptb[:, 7 * 128:8 * 128]),
               bank=6 + par)
        p.on("vector", lambda e: e.tensor_add(out=apt, in0=axt, in1=s.id_b[:]),
             wait=[vp], inc=[vp])
        for lvl in range(1, 6):
            s.pe_mm(self.pbank[binv][:, 128:256], axt, ax, bank=binv,
                    start=True, stop=True, wait=[vp])
            s.evac("vector", lambda e: e.tensor_copy(out=ax,
                                                     in_=self.pbank[binv][:, 128:256]),
                   bank=binv, sem=vp)
            s.pe_mm(self.pbank[binv][:, 256:384], ax, apt, bank=binv,
                    start=True, stop=True, wait=[vp])
            if lvl < 5:
                s.pe_tr(ptb[:, 7 * 128:8 * 128], ax, s.id_b[:], bank=6 + par,
                        wait=[vp])
                s.evac("vector", lambda e: e.tensor_copy(
                    out=axt, in_=ptb[:, 7 * 128:8 * 128]), bank=6 + par, sem=vp)
            s.evac("vector", lambda e: e.tensor_add(out=apt,
                                                    in0=self.pbank[binv][:, 256:384],
                                                    in1=apt), bank=binv, sem=vp)

    def emit_scan_head_B(self, w, pr, h, half, full, kslot, qslot, vslot, gslot,
                         ld_tgt):
        p, s = self.p, self
        pc = pr * 8
        par = h % 2
        vp = s.s_v0 if par == 0 else s.s_v1
        scp = s.s_c0 if par == 0 else s.s_c1
        g1 = [s.s_v, s.s_sc] if h < 2 else []
        bst = 4 if par == 0 else 1
        bs1 = 5 if par == 0 else 2
        bqm = 0 if par == 0 else 3
        ksl = [kslot[:, (2 * h + d) * 128:(2 * h + d + 1) * 128] for d in (0, 1)]
        qsl = [qslot[:, (2 * h + d) * 128:(2 * h + d + 1) * 128] for d in (0, 1)]
        nq = s.nsq[:, h * 8:h * 8 + 8]
        apt = s.APT6[:, h * 128:(h + 1) * 128]
        mut = s.MuT6[:, h * 128:(h + 1) * 128]
        ke = s.ketok6[:, h * DK:(h + 1) * DK]
        vo, wo_, uo, to = par * DV, par * DV, par * DV, par * DV
        bVs = s.bV[:, vo:vo + DV]
        wsb = s.w_sb[:, wo_:wo_ + DV]
        ubf = s.u_bf[:, uo:uo + DV]
        t2s = s.t2[:, to:to + DV]
        Sd = [s.S_sb[:, (h * 2 + d) * DV:(h * 2 + d + 1) * DV] for d in (0, 1)]
        p.on("vector", lambda e: e.tensor_scalar(
            out=bVs, in0=vslot[:, h * DV:(h + 1) * DV],
            scalar1=s.bwin[:, pc + h:pc + h + 1], scalar2=None, op0=ALU.mult),
             wait=([s.s_ld] if h < 2 else []) + [scp] + g1, inc=[vp])
        for c in (0, 1):
            rows = slice(c * 64, (c + 1) * 64)
            for d in (0, 1):
                s.pe_mm(self.pbank[bst][rows, 0:512], ksl[d][:, rows], Sd[d],
                        bank=bst, start=(d == 0), stop=(d == 1),
                        wait=[vp, scp], inc_pe=(d == 1))
            s.evac("vector", lambda e, rows=rows: e.scalar_tensor_tensor(
                out=wsb[rows, :], in0=self.pbank[bst][rows, 0:512],
                scalar=s.nbb[rows, h:h + 1], in1=bVs[rows, :],
                op0=ALU.mult, op1=ALU.add), bank=bst, sem=vp)
            s.pe_mm(self.pbank[bst][rows, 0:512],
                    apt[c * 64:(c + 1) * 64, c * 64:(c + 1) * 64],
                    wsb[rows, :], bank=bst, start=True, stop=True, wait=[vp])
            s.evac("vector", lambda e, rows=rows: e.tensor_copy(
                out=ubf[rows, :], in_=self.pbank[bst][rows, 0:512]), bank=bst, sem=vp)
            if full:
                for d in (0, 1):
                    s.pe_mm(self.pbank[bqm][rows, 0:512], qsl[d][:, rows], Sd[d],
                            bank=bqm, start=(d == 0), stop=(d == 1),
                            wait=[vp, scp] if d == 0 else [], inc_pe=(d == 1))
                s.evac("scalar", lambda e, rows=rows: e.activation(
                    out=t2s[rows, :], in_=self.pbank[bqm][rows, 0:512], func=AF.Copy,
                    scale=nq[rows, 7:8]), bank=bqm, wait=[vp], sem=scp)
                s.pe_mm(self.pbank[bqm][rows, 0:512],
                        mut[c * 64:(c + 1) * 64, c * 64:(c + 1) * 64],
                        ubf[rows, :], bank=bqm, start=True, stop=True,
                        wait=[vp, scp])
            for d in (0, 1):
                bank = bst if d == 0 else bs1
                s.pe_mm(self.pbank[bank][:, 0:512],
                        s.lamI[:, h * 256 + c * 128:h * 256 + (c + 1) * 128],
                        Sd[d], bank=bank, start=True, stop=False,
                        wait=[vp, scp] + g1)
                s.pe_mm(self.pbank[bank][:, 0:512],
                        ke[rows, d * 128:(d + 1) * 128], ubf[rows, :],
                        bank=bank, start=False, stop=True, inc_pe=True)
            for d in (0, 1):
                bank = bst if d == 0 else bs1
                eng = "vector" if d == 0 else "scalar"
                if eng == "vector":
                    s.evac(eng, lambda e, d=d, bank=bank: e.tensor_copy(
                        out=Sd[d], in_=self.pbank[bank][:, 0:512]), bank=bank, sem=vp)
                else:
                    s.evac(eng, lambda e, d=d, bank=bank: e.copy(
                        out=Sd[d], in_=self.pbank[bank][:, 0:512]), bank=bank, sem=scp)
            if full and c == 1:
                s.evac("vector", lambda e: e.tensor_add(out=t2s, in0=t2s,
                                                        in1=self.pbank[bqm][:, 0:512]),
                       bank=bqm, wait=[scp], sem=vp)
        if full:
            p.on("scalar", lambda e: e.activation(out=bVs, in_=t2s,
                                                  func=AF.Square,
                                                  accum_out=nq[:, 3:4]),
                 wait=[vp], inc=[scp])
            p.on("scalar", lambda e: e.activation(out=nq[:, 3:4], in_=nq[:, 3:4],
                                                  func=AF.Ln, scale=1.0 / DV,
                                                  bias=s.cpool[:, 0:1]), inc=[scp])
            p.on("scalar", lambda e: e.activation(out=nq[:, 3:4], in_=nq[:, 3:4],
                                                  func=AF.Exp, scale=-0.5), inc=[scp])
            p.on("vector", lambda e: e.scalar_tensor_tensor(
                out=t2s, in0=t2s, scalar=nq[:, 3:4], in1=s.normw_t[:],
                op0=ALU.mult, op1=ALU.mult), wait=[scp], inc=[vp])
            p.on("vector", lambda e: e.tensor_mul(
                out=s.oasm[:, h * DV:(h + 1) * DV], in0=t2s,
                in1=gslot[:, h * DV:(h + 1) * DV]),
                 wait=[s.s_gd] if (h == 0) else [], inc=[vp])

    def emit_phaseD(self):
        p, s = self.p, self
        self.emit_mod_bcast("ga")
        for tt in range(NPAIR):
            p.on("sync", lambda e, tt=tt: e.dma_start(out=s.otile[:], in_=s.o_d[tt]),
                 wait=[s.s_pe] if tt > 0 else [], inc=[(s.s_ld, 16)])
            for g in range(3):
                bank = 6 + (g % 2)
                ptb = self.pt[g % 2]
                for f8 in range(8):
                    fc = g * 8 + f8
                    s.pe_tr(ptb[:, f8 * 128:(f8 + 1) * 128],
                            s.otile[:, fc * 128:(fc + 1) * 128], s.id_b[:],
                            bank=bank, wait=[s.s_ld] if (g == 0 and f8 == 0) else [])
                eng = "vector" if (g % 2 == 0) else "scalar"
                for f8 in range(8):
                    fc = g * 8 + f8
                    dsl = s.oT[:, fc * TW + tt * 128: fc * TW + (tt + 1) * 128]
                    psl = ptb[:, f8 * 128:(f8 + 1) * 128]
                    if eng == "vector":
                        s.evac(eng, lambda e, d=dsl, ps=psl: e.tensor_copy(out=d, in_=ps),
                               bank=bank)
                    else:
                        s.evac(eng, lambda e, d=dsl, ps=psl: e.copy(out=d, in_=ps),
                               bank=bank)
        for dt in range(4):
            for kcv in range(24):
                p.on("sync", lambda e, dt=dt, kcv=kcv: e.dma_start(
                    out=s.pan_f[:, (kcv % 2) * 2048:(kcv % 2) * 2048 + 512],
                    in_=s.wo[kcv * 128:(kcv + 1) * 128, dt * 512:(dt + 1) * 512]),
                     wait=[s.s_w], inc=[(s.s_ld, 16)])
                p.on("vector", lambda e, kcv=kcv: e.tensor_copy(
                    out=s.wopan[:, kcv * 512:(kcv + 1) * 512],
                    in_=s.pan_f[:, (kcv % 2) * 2048:(kcv % 2) * 2048 + 512]),
                     wait=[s.s_ld, s.s_pe], inc=[s.s_w])
            for tg in range(2):
                for tt4 in range(4):
                    tt = tg * 4 + tt4
                    bank = tt4
                    for kcv in range(24):
                        s.pe_mm(self.pbank[bank][:, 0:512],
                                s.oT[:, kcv * TW + tt * 128: kcv * TW + (tt + 1) * 128],
                                s.wopan[:, kcv * 512:(kcv + 1) * 512],
                                bank=bank, start=(kcv == 0), stop=(kcv == 23),
                                wait=([s.s_w] + ([s.s_v, s.s_sc] if (dt == 0 and tg == 0 and tt4 == 0) else []))
                                if kcv == 0 else [],
                                inc_pe=(kcv == 23))
                for tt4 in range(4):
                    tt = tg * 4 + tt4
                    bank = tt4
                    p.on("sync", lambda e, tt=tt, dt=dt: e.dma_start(
                        out=s.res_t[:, 0:512],
                        in_=s.xw[1, 128 + tt * 128:256 + tt * 128, dt * 512:(dt + 1) * 512]),
                         wait=[s.s_v], inc=[(s.s_ld, 16)])
                    s.evac("vector", lambda e, bank=bank, dt=dt: e.tensor_mul(
                        out=s.res_t[:, 512:1024], in0=self.pbank[bank][:, 0:512],
                        in1=s.bcA[:, dt * 512:(dt + 1) * 512]), bank=bank,
                           wait=[s.s_ld, s.s_gd])
                    p.on("vector", lambda e: e.tensor_add(
                        out=s.res_t[:, 512:1024], in0=s.res_t[:, 512:1024],
                        in1=s.res_t[:, 0:512]), inc=[s.s_v])
                    p.on("gpsimd", lambda e, tt=tt, dt=dt: e.dma_start(
                        out=s.x1_d[tt, :, dt * 512:(dt + 1) * 512],
                        in_=s.res_t[:, 512:1024]), wait=[s.s_v], inc=[(s.s_gd, 16)])
        self.emit_mod_bcast("b")
        for tt in range(NPAIR):
            self.emit_ln_tile(s.x1_d[tt], s.h2T, TW, tt * 128,
                              last_readers=[s.s_v, s.s_sc, s.s_gd])

    def emit_ffnA(self):
        p, s = self.p, self

        def stage_ffn(hc):
            po = (hc % 2) * 2048
            p.on("sync", lambda e, hc=hc, po=po: e.dma_start(
                out=s.pan_f[:, po:po + 2048],
                in_=s.wgt[:, hc * 128:(hc + 1) * 128]
                .rearrange("(kc p) m -> p kc m", p=128)),
                 wait=[s.s_pe], inc=[(s.s_ld, 16)])
            p.on("vector", lambda e, po=po: e.tensor_copy(
                out=s.pan_b[:, po:po + 2048], in_=s.pan_f[:, po:po + 2048]),
                 wait=[s.s_ld, s.s_pe], inc=[s.s_w])
            p.on("sync", lambda e, hc=hc, po=po: e.dma_start(
                out=s.fpan_f[:, po:po + 2048],
                in_=s.wut[:, hc * 128:(hc + 1) * 128]
                .rearrange("(kc p) m -> p kc m", p=128)),
                 wait=[s.s_pe], inc=[(s.s_ld, 16)])
            p.on("vector", lambda e, po=po: e.tensor_copy(
                out=s.fpan_b[:, po:po + 2048], in_=s.fpan_f[:, po:po + 2048]),
                 wait=[s.s_ld, s.s_pe], inc=[s.s_w])
        stage_ffn(0)
        stage_ffn(1)
        for hc in range(64):
            po = (hc % 2) * 2048
            gb0 = 0 if hc % 2 == 0 else 2
            gsl = s.gu_sb[:, 0:1024] if hc % 2 == 0 else s.gu_sb[:, 1024:2048]
            for mt in (0, 1):
                for kc in range(KC):
                    s.pe_mm(self.pbank[gb0 + mt][:, 0:512],
                            s.pan_b[:, po + kc * 128:po + (kc + 1) * 128],
                            s.h2T[:, kc * TW + mt * 512: kc * TW + (mt + 1) * 512],
                            bank=gb0 + mt, start=(kc == 0), stop=(kc == 15),
                            wait=([s.s_w] + ([s.s_v, s.s_sc] if hc == 0 else []))
                            if kc == 0 else [], inc_pe=(kc == 15))
            for mt in (0, 1):
                for kc in range(KC):
                    s.pe_mm(self.pbank[4 + mt][:, 0:512],
                            s.fpan_b[:, po + kc * 128:po + (kc + 1) * 128],
                            s.h2T[:, kc * TW + mt * 512: kc * TW + (mt + 1) * 512],
                            bank=4 + mt, start=(kc == 0), stop=(kc == 15),
                            wait=[s.s_w] if kc == 0 else [], inc_pe=(kc == 15))
            if hc + 2 < 64:
                stage_ffn(hc + 2)
            for mt in (0, 1):
                s.evac("scalar", lambda e, mt=mt, gb0=gb0, gsl=gsl: e.activation(
                    out=gsl[:, mt * 512:(mt + 1) * 512],
                    in_=self.pbank[gb0 + mt][:, 0:512], func=AF.Silu), bank=gb0 + mt,
                       wait=[s.s_gd])
            for mt in (0, 1):
                s.evac("vector", lambda e, mt=mt, gsl=gsl: e.tensor_mul(
                    out=gsl[:, mt * 512:(mt + 1) * 512],
                    in0=gsl[:, mt * 512:(mt + 1) * 512],
                    in1=self.pbank[4 + mt][:, 0:512]), bank=4 + mt, wait=[s.s_sc])
            p.on("gpsimd", lambda e, hc=hc, gsl=gsl: e.dma_start(
                out=s.gu_d[hc * 128:(hc + 1) * 128, :], in_=gsl),
                 wait=[s.s_v], inc=[(s.s_gd, 16)])

    def emit_ffnB(self):
        p, s = self.p, self
        self.emit_mod_bcast("gb", scr=self.res2_t)
        for dt in range(4):
            for cb in range(8):
                for hh in (0, 1):
                    p.on("sync", lambda e, dt=dt, cb=cb, hh=hh: e.dma_start(
                        out=s.pan_f[:, hh * 2048:(hh + 1) * 2048]
                            .rearrange("p (kc m) -> p kc m", m=64),
                        in_=s.wdt[hh * 4096:(hh + 1) * 4096,
                                  dt * 512 + cb * 64: dt * 512 + (cb + 1) * 64]
                        .rearrange("(kc p) m -> p kc m", p=128)),
                         wait=[s.s_w], inc=[(s.s_ld, 16)])
                    p.on("vector", lambda e, cb=cb, hh=hh: e.tensor_copy(
                        out=s.wdpan[:]
                        .rearrange("p (kc m) -> p kc m", m=512)
                        [:, hh * 32:(hh + 1) * 32, cb * 64:(cb + 1) * 64],
                        in_=s.pan_f[:, hh * 2048:(hh + 1) * 2048]
                        .rearrange("p (kc m) -> p kc m", m=64)),
                         wait=[s.s_ld, s.s_pe], inc=[s.s_w])
            for tt in range(NPAIR):
                bank = (dt * 8 + tt) % 6
                gslot = s.gucol[:, (tt % 2) * 8192:(tt % 2 + 1) * 8192]
                p.on("sync", lambda e, tt=tt, gslot=gslot: e.dma_start(
                    out=gslot, in_=s.gu_d[:, tt * 128:(tt + 1) * 128]
                    .rearrange("(kc p) m -> p kc m", p=128)),
                     wait=[s.s_pe] if tt >= 2 else ([s.s_gd] if dt == 0 else []),
                     inc=[(s.s_ld, 16)])
                for kc in range(64):
                    s.pe_mm(self.pbank[bank][:, 0:512],
                            gslot[:, kc * 128:(kc + 1) * 128],
                            s.wdpan[:, kc * 512:(kc + 1) * 512],
                            bank=bank, start=(kc == 0), stop=(kc == 63),
                            wait=[s.s_w, s.s_ld] if kc == 0 else [],
                            inc_pe=(kc == 63))
                ro = (tt % 2) * 1024
                p.on("sync", lambda e, tt=tt, dt=dt, ro=ro: e.dma_start(
                    out=s.res2_t[:, ro:ro + 512],
                    in_=s.x1_d[tt, :, dt * 512:(dt + 1) * 512]),
                     wait=[s.s_v], inc=[(s.s_ld, 16)])
                s.evac("vector", lambda e, bank=bank, dt=dt, ro=ro: e.tensor_mul(
                    out=s.res2_t[:, ro + 512:ro + 1024], in0=self.pbank[bank][:, 0:512],
                    in1=s.bcA[:, dt * 512:(dt + 1) * 512]), bank=bank,
                       wait=[s.s_ld, s.s_gd])
                p.on("vector", lambda e, ro=ro: e.tensor_add(
                    out=s.res2_t[:, ro + 512:ro + 1024], in0=s.res2_t[:, ro + 512:ro + 1024],
                    in1=s.res2_t[:, ro:ro + 512]), inc=[s.s_v])
                p.on("gpsimd", lambda e, tt=tt, dt=dt, ro=ro: e.dma_start(
                    out=s.out[tt * 128:(tt + 1) * 128, dt * 512:(dt + 1) * 512],
                    in_=s.res2_t[:, ro + 512:ro + 1024]), wait=[s.s_v], inc=[(s.s_gd, 16)])

    def emit_final(self):
        self.p.on("gpsimd", None, wait=[self.s_gd])
        self.p.on("sync", None, wait=[self.s_gd])


def stage_inputs(inputs):
    x = inputs["L0_attn_hidden_states"]
    ident = np.eye(128, dtype=np.float32)
    identb = bfarr(np.eye(128))
    u64 = np.triu(np.ones((64, 64), np.float32))
    cumu = np.concatenate([u64, u64], 0)
    ms = np.zeros((128, 128), np.float32)
    mi = np.zeros((128, 128), np.float32)
    ms[0:64, 0:64] = np.tril(np.ones((64, 64)), -1)
    ms[64:128, 64:128] = np.tril(np.ones((64, 64)), -1)
    mi[0:64, 0:64] = np.tril(np.ones((64, 64)), 0)
    mi[64:128, 64:128] = np.tril(np.ones((64, 64)), 0)
    onesf = np.ones((1, 128), np.float32)
    constp = np.zeros((128, 8), np.float32)
    constp[:, 0] = EPS
    constp[:, 1] = np.log(DK ** -0.5)
    constp[:, 2] = 1.0
    constp[:, 3] = 0.25
    convw = np.zeros((128, 192), np.float32)
    cw = np.concatenate([inputs["L0_attn_conv_wq"], inputs["L0_attn_conv_wk"],
                         inputs["L0_attn_conv_wv"]], 0)
    for fc in range(48):
        convw[:, fc * 4:(fc + 1) * 4] = cw[fc * 128:(fc + 1) * 128, :]
    alog = np.zeros((1, 8), np.float32); alog[0, :6] = inputs["L0_attn_A_log"]
    dtb = np.zeros((1, 8), np.float32); dtb[0, :6] = inputs["L0_attn_dt_bias"]
    wab = np.zeros((D, 16), np.float32)
    wab[:, 0:6] = inputs["L0_attn_Wa"]
    wab[:, 8:14] = inputs["L0_attn_Wb"]
    shared = dict(
        wq=np.ascontiguousarray(inputs["L0_attn_Wq"]),
        wk=np.ascontiguousarray(inputs["L0_attn_Wk"]),
        wv=np.ascontiguousarray(inputs["L0_attn_Wv"]),
        wab=wab,
        wg=np.ascontiguousarray(inputs["L0_attn_Wg"]),
        wo=np.ascontiguousarray(inputs["L0_attn_Wo"]),
        wmod=np.ascontiguousarray(inputs["L0_wmod"]),
        wgt=np.ascontiguousarray(inputs["L0_ffn_w_gate"].T),
        wut=np.ascontiguousarray(inputs["L0_ffn_w_up"].T),
        wdt=np.ascontiguousarray(inputs["L0_ffn_w_down"].T),
        convw=convw, alog=alog, dtbv=dtb,
        normw=np.ascontiguousarray(inputs["L0_attn_norm_weight"].reshape(1, DV)),
        ident=ident, identb=identb, cumu=cumu, masks_c=ms, maski_c=mi,
        onesf=onesf, constp=constp,
    )
    z128 = np.zeros((128, D), np.float32)
    maps = []
    for core in range(8):
        b, th = core // 2, core % 2
        xb = x[b]
        xpre = np.concatenate([z128, xb[0:1024]], 0)
        base = th * 1024
        halo = z128 if th == 0 else xb[base - 128:base]
        xmain = np.concatenate([halo, xb[base:base + 1024]], 0)
        m = dict(shared)
        m["xw"] = np.ascontiguousarray(np.stack([xpre, xmain], 0))
        m["c_r"] = np.ascontiguousarray(inputs["L0_c"][b].reshape(16, 128))
        m["maskpre"] = np.full((128, 1), float(th == 1), np.float32)
        m["maskhalo"] = np.full((128, 1), float(th == 1), np.float32)
        maps.append(m)
    return maps


_CACHE = {}


def kernel(**inputs):
    if "nc" not in _CACHE:
        _CACHE["nc"] = KB().build()
    nc = _CACHE["nc"]
    maps = stage_inputs(inputs)
    res = run_bass_kernel_spmd(nc, maps, core_ids=list(range(8)))
    out = np.empty((B, T, D), np.float32)
    for core in range(8):
        b, th = core // 2, core % 2
        out[b, th * 1024:(th + 1) * 1024] = res.results[core]["out"]
    return out
